# version 24 (final)
# baseline (speedup 1.0000x reference)
import math
import numpy as np
import ml_dtypes
from contextlib import ExitStack

import concourse.bass as bass
import concourse.mybir as mybir
from concourse.bass_utils import run_bass_kernel_spmd

F32 = mybir.dt.float32
BF16 = mybir.dt.bfloat16
AF = mybir.ActivationFunctionType
ALU = mybir.AluOpType
AX = mybir.AxisListType
BF = ml_dtypes.bfloat16

H, DP, DH, DTT, D, DFF = 16, 16, 64, 273, 1024, 4096
HB = H * 65
SB3 = 3 * HB
ENGS = ("sync", "gpsimd", "vector", "scalar", "tensor")
DTC = [(0, 128), (128, 128), (256, 17)]


class KB:
    def __init__(self, nc, ctx):
        self.nc = nc
        self.ctx = ctx
        self.ops = {e: [] for e in ENGS}
        self.semn = {}
        self.sems = {}
        self.pb_open = [False] * 8
        self.freeb = {}
        for s in ("sV", "sC", "sT", "sGc", "sS"):
            self.sems[s] = ctx.enter_context(nc.semaphore(s))

    def use(self, name, *evs):
        self.freeb.setdefault(name, []).extend(e for e in evs if e is not None)

    def takefree(self, name):
        evs = self.freeb.get(name, [])
        self.freeb[name] = []
        return evs

    def sb(self, name, shape, dt=BF16, ctx=None):
        return (ctx or self.ctx).enter_context(self.nc.sbuf_tensor(name, shape, dt))

    def bump(self, s, amt):
        self.semn[s] = self.semn.get(s, 0) + amt
        return (s, self.semn[s])

    def now(self, s):
        return (s, self.semn.get(s, 0))

    def getsem(self, name):
        if name not in self.sems:
            self.sems[name] = self.ctx.enter_context(self.nc.semaphore(name))
        return self.sems[name]

    def emit(self, engine, fn, waits=(), raw=False):
        seen = {}
        for w in waits:
            if w is None:
                continue
            s, v = w
            if v > seen.get(s, -1):
                seen[s] = v
        out = list(seen.items())
        for s, _v in out:
            self.getsem(s)
        sems = self.sems
        def op(eng, fn=fn, waits=tuple(out)):
            for (s, v) in waits:
                if v > 0:
                    eng.wait_ge(sems[s], v)
            fn(eng)
        self.ops[engine].append(op)

    def fence(self, engines, waits):
        for e in engines:
            self.emit(e, lambda eng: None, waits=waits)

    def dma(self, q, out, in_, waits=(), sem=None):
        assert sem is not None, "dma needs a stream sem name"
        self.getsem(sem)
        ev = self.bump(sem, 16)
        sems = self.sems
        def fn(eng, out=out, in_=in_, sem=sem):
            eng.dma_start(out=out, in_=in_).then_inc(sems[sem], 16)
        self.emit(q, fn, waits)
        return ev

    def act(self, engine, build, waits=(), amt=1):
        sem = {"vector": "sV", "scalar": "sC", "tensor": "sT",
               "gpsimd": "sGc", "sync": "sS"}[engine]
        ev = self.bump(sem, amt)
        sems = self.sems
        def fn(eng, build=build, sem=sem, amt=amt):
            build(eng).then_inc(sems[sem], amt)
        self.emit(engine, fn, waits)
        return ev

    def mm(self, bank, build, *, first, last, waits=()):
        if first:
            assert not self.pb_open[bank], f"bank {bank} already open"
            self.pb_open[bank] = True
            waits = list(waits) + self.takefree(f"pb{bank}")
        else:
            assert self.pb_open[bank], f"bank {bank} not open"
        if last:
            self.pb_open[bank] = False
        return self.act("tensor",
                        lambda e, b=build, f=first, l=last: b(e, f, l), waits)

    def tap_dma(self, stream, out, in_, waits=()):
        if not hasattr(self, "_last_tap"):
            self._last_tap = {}
        prev = self._last_tap.get(stream)
        ev = self.dma("gpsimd", out, in_, waits=list(waits) + [prev], sem=stream)
        self._last_tap[stream] = ev
        return ev

    def trans(self, bank, build, waits=()):
        waits = list(waits) + self.takefree(f"pb{bank}")
        return self.act("tensor", build, waits)


def build_program(T=512, mode="full", fused_cc=False, debug_taps=()):
    assert T % 512 == 0
    NT = T // 128 + 1
    NCH = T // 512
    nc = bass.Bass("TRN2", target_bir_lowering=False)
    full = (mode == "full")

    xT_d = nc.dram_tensor("xT", [D, NT * 128], F32, kind="ExternalInput")
    wqk_d = nc.dram_tensor("wqk", [D, 512], F32, kind="ExternalInput")
    wva_d = nc.dram_tensor("wva", [D, HB], F32, kind="ExternalInput")
    wka_d = nc.dram_tensor("wka", [D, 272], F32, kind="ExternalInput")
    bkra_d = nc.dram_tensor("bkra", [1, 272], F32, kind="ExternalInput")
    bvar_d = nc.dram_tensor("bvar", [1, HB], F32, kind="ExternalInput")
    master_out_d = nc.dram_tensor("master_out", [128, SB3], F32, kind="ExternalOutput")
    vaug_dram = nc.dram_tensor("vaug_scr", [NT, 128, HB], BF16)
    snap_dram = nc.dram_tensor("snap_scr", [max(2 * NCH - 1, 1), 128, SB3], BF16)
    if full:
        xTM_d = nc.dram_tensor("xTM", [T, D], F32, kind="ExternalInput")
        bqk_d = nc.dram_tensor("bqk", [512, 1], F32, kind="ExternalInput")
        wup_d = nc.dram_tensor("wup_sw", [128, 8 * DFF], F32, kind="ExternalInput")
        wdn_d = nc.dram_tensor("wdn_sw", [128, 32 * D], F32, kind="ExternalInput")
        mcA_d = nc.dram_tensor("mcA", [128, 128], F32, kind="ExternalInput")
        mbC_d = nc.dram_tensor("mbC", [128, 128], F32, kind="ExternalInput")
        mbH_d = nc.dram_tensor("mbH", [128, 128], F32, kind="ExternalInput")
        mbH0_d = nc.dram_tensor("mbH0", [128, 128], F32, kind="ExternalInput")
        mskv_d = nc.dram_tensor("mskv", [128, 4], F32, kind="ExternalInput")
        iden_d = nc.dram_tensor("iden", [128, 128], F32, kind="ExternalInput")
        repm_d = nc.dram_tensor("repm", [16, 3 * 128], F32, kind="ExternalInput")
        out_d = nc.dram_tensor("out", [T, D], F32, kind="ExternalOutput")
        r1_dram = nc.dram_tensor("r1_scr", [T, D], F32)
        wtu_dram = nc.dram_tensor("wtu_scr", [128, 8 * DFF], BF16)
        wtd_dram = nc.dram_tensor("wtd_scr", [128, 32 * D], BF16)
        if fused_cc:
            cc_in_d = nc.dram_tensor("cc_in", [128, SB3], F32)
            cc_out_d = nc.dram_tensor("cc_out", [4, 128, SB3], F32)
        else:
            peers_d = nc.dram_tensor("peers", [4, 128, SB3], F32, kind="ExternalInput")

    taps = {}
    def mktap(name, shape):
        if name in debug_taps:
            taps[name] = nc.dram_tensor("tap_" + name, shape, F32, kind="ExternalOutput")
    mktap("vaug", [NT, 128, HB])
    mktap("phik", [NT - 1, 128, H * DTT])
    if full:
        mktap("qkfm", [NCH, 128, 4 * 640])
        mktap("phiqq", [NCH, 4, 128, 8 * 512])
        mktap("saug", [2 * NCH, 128, SB3])
        mktap("sstart", [128, SB3])
        mktap("y", [T, D])
        mktap("r1", [T, D])
        mktap("xn", [T, D])
        mktap("xnf", [NCH, 128, 8 * 512])
        mktap("hf", [NCH, 128, 32 * 512])
        mktap("wtu", [128, 8 * DFF])
        mktap("wtd", [128, 32 * D])

    ctx = ExitStack()
    with ctx:
        K = KB(nc, ctx)
        sb = K.sb
        use = K.use
        takefree = K.takefree
        freeb = K.freeb

        wqk_s = sb("wqk_s", [128, 8 * 512])
        ones_row = sb("ones_r", [1, 128])
        allon_s = sb("allon_s", [128, 128], F32)
        xfm_s = [sb(f"xfm{b}", [128, 8 * 640]) for b in range(2)]
        vring_s = [sb(f"vring{b}", [128, HB]) for b in range(5)]
        pb = [ctx.enter_context(nc.psum_tensor(f"pb{i}", [128, 512], F32)) for i in range(8)]

        for k in range(8):
            K.dma("gpsimd", wqk_s[:, 512 * k:512 * (k + 1)], wqk_d[128 * k:128 * (k + 1), :],
                  sem="cst")
        ev_ones = K.act("vector", lambda e: e.memset(ones_row[:], 1.0))
        ev_allon = K.act("vector", lambda e: e.memset(allon_s[:], 1.0))
        if full:
            mcA_s = sb("mcA_s", [128, 128])
            mbC_s = sb("mbC_s", [128, 128])
            mbH_s = sb("mbH_s", [128, 128])
            mbH0_s = sb("mbH0_s", [128, 128])
            mskv_s = sb("mskv_s", [128, 4], F32)
            idenb_s = sb("idenb_s", [128, 128])
            idmask_s = sb("idmask_s", [128, 128], F32)
            repm_s = sb("repm_s", [16, 3 * 128])
            bqk_s = sb("bqk_s", [128, 4], F32)
            K.dma("gpsimd", mcA_s[:], mcA_d[:], sem="cst")
            K.dma("gpsimd", mbC_s[:], mbC_d[:], sem="cst")
            K.dma("gpsimd", mbH_s[:], mbH_d[:], sem="cst")
            K.dma("gpsimd", mbH0_s[:], mbH0_d[:], sem="cst")
            K.dma("gpsimd", idenb_s[:], iden_d[:], sem="cst")
            K.dma("gpsimd", repm_s[:], repm_d[:], sem="cst")
            K.dma("sync", idmask_s[:], iden_d[:], sem="css")
            K.dma("sync", mskv_s[:], mskv_d[:], sem="css")
            for f in range(4):
                K.dma("sync", bqk_s[:, f:f + 1], bqk_d[128 * f:128 * (f + 1), :], sem="css")
        ev_cst = K.now("cst")
        ev_css = K.now("css")

        ctx_p1 = ExitStack()
        tern_items = []
        tref = {"wtu": None, "wtd": None}
        if full:
            tld_s = [sb(f"tld{b}", [128, 2048], F32, ctx=ctx_p1) for b in range(2)]
            tsg_s = sb("tsg_s", [128, 2048], F32, ctx=ctx_p1)
            tsgb_s = sb("tsgb_s", [128, 2048], ctx=ctx_p1)
            tsnb_s = sb("tsnb_s", [128, 2048], ctx=ctx_p1)
            twt_s = [sb(f"twt{b}", [128, 2048], ctx=ctx_p1) for b in range(2)]
            tstat_s = sb("tstat_s", [128, 4], F32, ctx=ctx_p1)
            talpha_s = sb("talpha_s", [128, 2], F32, ctx=ctx_p1)

            def mk_tern_items(w_src, wt_dst, ncols, acol, key):
                nch = ncols // 2048
                st = {}

                def alpha_start():
                    st["prev"] = K.act("vector", lambda e: e.memset(tstat_s[:, 0:1], 0.0))
                tern_items.append(alpha_start)

                def alpha_batch(i):
                    b = i % 2
                    evl = K.dma("sync", tld_s[b][:], w_src[:, 2048 * i:2048 * (i + 1)],
                                waits=takefree(f"tld{b}"), sem=f"tld{b}")
                    evr = K.act("scalar", lambda e, b=b: e.activation(
                        tsg_s[:], tld_s[b][:], AF.Abs,
                        accum_out=tstat_s[:, 1:2]),
                        waits=[evl, st["prev"]] + takefree("tsg"))
                    use(f"tld{b}", evr)
                    use("tsg", evr)
                    st["prev"] = K.act("vector", lambda e: e.tensor_add(
                        tstat_s[:, 0:1], tstat_s[:, 0:1], tstat_s[:, 1:2]), waits=[evr])
                for i in range(nch):
                    tern_items.append(lambda i=i: alpha_batch(i))

                def alpha_fin():
                    evmm = K.mm(7, lambda e, f, l: e.matmul(
                        pb[7][:, 0:1], allon_s[:], tstat_s[:, 0:1], start=f, stop=l),
                        first=True, last=True, waits=[st["prev"], ev_allon])
                    st["th"] = K.act("vector", lambda e: e.tensor_scalar(
                        talpha_s[:, acol:acol + 1], pb[7][:, 0:1],
                        0.5 / (128.0 * ncols), None, ALU.mult), waits=[evmm])
                    use("pb7", st["th"])
                tern_items.append(alpha_fin)

                def tern_batch(i):
                    b = i % 2
                    evl = K.dma("sync", tld_s[b][:], w_src[:, 2048 * i:2048 * (i + 1)],
                                waits=takefree(f"tld{b}"), sem=f"tld{b}")
                    evab = K.act("scalar", lambda e, b=b: e.activation(
                        tsg_s[:], tld_s[b][:], AF.Abs), waits=[evl] + takefree("tsg"))
                    evg = K.act("vector", lambda e: e.tensor_scalar(
                        tsgb_s[:], tsg_s[:], talpha_s[:, acol:acol + 1], None, ALU.is_gt),
                        waits=[evab, st["th"]] + takefree("tsgb"))
                    use("tsg", evg)
                    evsg = K.act("scalar", lambda e, b=b: e.sign(tsnb_s[:], tld_s[b][:]),
                                 waits=[evab] + takefree("tsnb"))
                    use(f"tld{b}", evsg)
                    evm = K.act("vector", lambda e, b=b: e.tensor_mul(
                        twt_s[b][:], tsnb_s[:], tsgb_s[:]),
                        waits=[evsg, evg] + takefree(f"twt{b}"))
                    use("tsgb", evm)
                    use("tsnb", evm)
                    evd = K.dma("gpsimd", wt_dst[:, 2048 * i:2048 * (i + 1)], twt_s[b][:],
                                waits=[evm], sem=f"twr{b}")
                    use(f"twt{b}", evd)
                    tref[key] = evd
                for i in range(nch):
                    tern_items.append(lambda i=i: tern_batch(i))

            mk_tern_items(wup_d, wtu_dram, 8 * DFF, 0, "wtu")
            mk_tern_items(wdn_d, wtd_dram, 32 * D, 1, "wtd")

        def emit_tern(n):
            for _ in range(n):
                if tern_items:
                    tern_items.pop(0)()

        ktm_s = [sb(f"ktm{b}", [128, 272], ctx=ctx_p1) for b in range(2)]
        wka_s = sb("wka_s", [128, 8 * 272], ctx=ctx_p1)
        phik_s = [sb(f"phik{b}", [128, H * DTT], ctx=ctx_p1) for b in range(4)]
        master_s = sb("master_s", [128, SB3], F32, ctx=ctx_p1)
        wva_s = sb("wva_s", [128, 8 * HB], ctx=ctx_p1)
        bkra_s = sb("bkra_s", [1, 272], ctx=ctx_p1)
        bvar_s = sb("bvar_s", [1, HB], ctx=ctx_p1)
        for k in range(8):
            K.dma("gpsimd", wva_s[:, HB * k:HB * (k + 1)], wva_d[128 * k:128 * (k + 1), :],
                  sem="cp1")
        for k in range(8):
            K.dma("gpsimd", wka_s[:, 272 * k:272 * (k + 1)],
                  wka_d[128 * k:128 * (k + 1), :], sem="cp1")
        K.dma("gpsimd", bkra_s[:], bkra_d[:], sem="cp1")
        ev_bvar = K.dma("gpsimd", bvar_s[:], bvar_d[:], sem="cp1")

        ev_mz = K.act("vector", lambda e: e.memset(master_s[:], 0.0))
        ev_phi1 = []
        for b in range(4):
            ap = phik_s[b][:].rearrange("p (h r) -> p h r", h=H)[:, :, 272:273]
            ev_phi1.append(K.act("vector", lambda e, ap=ap: e.memset(ap, 1.0)))

        def load_xfm(buf, t0, ntl):
            w = takefree(f"xfm{buf}")
            for k in range(8):
                ev = K.dma("gpsimd",
                           xfm_s[buf][:, 640 * k:640 * k + 128 * ntl],
                           xT_d[128 * k:128 * (k + 1), 128 * t0:128 * (t0 + ntl)],
                           waits=w if k == 0 else (), sem=f"xf{buf}")
            return [ev]

        def proj_kv_tile(buf, tl, tabs, ev_x):
            par = proj_kv_tile.count % 2
            proj_kv_tile.count += 1
            slot = tabs % 5
            wk_v = takefree(f"vring{slot}")
            wk_k = takefree(f"ktm{par}")
            for k in range(8):
                fi = (k == 0)
                lhs = xfm_s[buf][:, 640 * k + 128 * tl: 640 * k + 128 * (tl + 1)]
                w0 = (list(ev_x) + [ev_cst, ev_css, K.now("cp1")] if k == 0 else [])
                K.mm(0, lambda e, f, l, k=k, lhs=lhs: e.matmul(
                    pb[0][:, 0:272], lhs,
                    wka_s[:, 272 * k: 272 * (k + 1)], start=f, stop=l),
                    first=fi, last=False, waits=w0)
                K.mm(1, lambda e, f, l, k=k, lhs=lhs: e.matmul(
                    pb[1][:, 0:512], lhs,
                    wva_s[:, HB * k: HB * k + 512], start=f, stop=l),
                    first=fi, last=False)
                K.mm(2, lambda e, f, l, k=k, lhs=lhs: e.matmul(
                    pb[2][:, 0:512], lhs,
                    wva_s[:, HB * k + 512: HB * k + 1024], start=f, stop=l),
                    first=fi, last=False)
                K.mm(3, lambda e, f, l, k=k, lhs=lhs: e.matmul(
                    pb[3][:, 0:16], lhs,
                    wva_s[:, HB * k + 1024: HB * k + 1040], start=f, stop=l),
                    first=fi, last=False)
            ev0 = K.mm(0, lambda e, f, l: e.matmul(
                pb[0][:, 0:272], ones_row[:], bkra_s[:], start=f, stop=l),
                first=False, last=True, waits=[ev_ones])
            ev1 = K.mm(1, lambda e, f, l: e.matmul(
                pb[1][:, 0:512], ones_row[:], bvar_s[:, 0:512], start=f, stop=l),
                first=False, last=True)
            ev2 = K.mm(2, lambda e, f, l: e.matmul(
                pb[2][:, 0:512], ones_row[:], bvar_s[:, 512:1024], start=f, stop=l),
                first=False, last=True)
            ev3 = K.mm(3, lambda e, f, l: e.matmul(
                pb[3][:, 0:16], ones_row[:], bvar_s[:, 1024:1040], start=f, stop=l),
                first=False, last=True)
            evk = K.act("scalar", lambda e: e.activation(
                ktm_s[par][:], pb[0][:, 0:272], AF.Identity), waits=[ev0] + wk_k)
            evva = K.act("scalar", lambda e: e.activation(
                vring_s[slot][:, 0:512], pb[1][:, 0:512], AF.Identity),
                waits=[ev1] + wk_v)
            evvb = K.act("scalar", lambda e: e.activation(
                vring_s[slot][:, 512:1024], pb[2][:, 0:512], AF.Identity), waits=[ev2])
            evvc = K.act("scalar", lambda e: e.activation(
                vring_s[slot][:, 1024:1040], pb[3][:, 0:16], AF.Identity), waits=[ev3])
            evs_v = [evva, evvb, evvc]
            use("pb0", evk)
            use("pb1", evva)
            use("pb2", evvb)
            use("pb3", evvc)
            evd = K.dma("sync", vaug_dram[tabs], vring_s[slot][:], waits=evs_v,
                        sem=f"vr{slot}")
            use(f"vring{slot}", evd)
            if "vaug" in taps:
                use(f"vring{slot}", K.tap_dma("tapv", taps["vaug"][tabs],
                                              vring_s[slot][:], waits=[evd]))
            return par, slot, evk, evs_v
        proj_kv_tile.count = 0

        def build_phik(pbuf, par_ktm, ev_k, tabs):
            wk = takefree(f"phik{pbuf}") + [ev_phi1[pbuf]]
            kt = ktm_s[par_ktm]
            ph = phik_s[pbuf][:].rearrange("p (h r) -> p h r", h=H)
            outer_all = ph[:, :, 0:272].rearrange("p h (i j) -> p h i j", i=17)
            kh_i = kt[:].rearrange("p (h i) -> p h i", i=17)
            ev_o = []
            for h0, h1, eng in ((0, 16, "vector"),):
                nh = h1 - h0
                oo = outer_all[:, h0:h1]
                ii = kh_i[:, h0:h1].unsqueeze(3).broadcast_to([128, nh, 17, 16])
                jj = kh_i[:, h0:h1, 0:16].unsqueeze(2).broadcast_to([128, nh, 17, 16])
                ev_o.append(K.act(eng, lambda e, oo=oo, ii=ii, jj=jj: e.tensor_tensor(
                    oo, ii, jj, ALU.mult), waits=[ev_k] + wk))
            ev_kc = None
            use(f"ktm{par_ktm}", *ev_o)
            if "phik" in taps:
                use(f"phik{pbuf}", K.tap_dma("tapk", taps["phik"][tabs - 1],
                                             phik_s[pbuf][:], waits=ev_o))
            return ev_o

        ev_x_cur = load_xfm(0, 0, 5)
        ev_master_done = None
        ev_ds_evac_prev = []
        ev_snap_prev = None
        for c in range(NCH):
            buf = c % 2
            ev_x = ev_x_cur
            tiles = ([0] if c == 0 else []) + [4 * c + tl for tl in range(1, 5)]
            phik_evs = {}
            vslot = {}
            for tabs in tiles:
                tl = tabs - 4 * c
                par, slot, evk, evs_v = proj_kv_tile(buf, tl, tabs, ev_x)
                vslot[tabs] = (slot, evs_v)
                if tabs > 0:
                    pbuf = (tabs - 1) % 4
                    phik_evs[tabs] = (pbuf, build_phik(pbuf, par, evk, tabs))
            emit_tern(5)
            ev_mm_all = None
            for hc in range(2):
                tiles_half = [4 * c + 2 * hc + tl for tl in range(1, 3)]
                for g in range(4):
                    gg = 4 * hc + g
                    b0 = 4 if gg % 2 == 0 else 0
                    opened = {b0: False, b0 + 1: False, b0 + 2: False}
                    for tabs in tiles_half:
                        pbuf, evp = phik_evs[tabs]
                        slot, evs_v = vslot[tabs]
                        tile_first = True
                        for hg in range(4):
                            h = 4 * g + hg
                            for cdt, (r0, rn) in enumerate(DTC):
                                bank = b0 + cdt
                                first = not opened[bank]
                                last = (tabs == tiles_half[-1]) and (hg == 3)
                                w = []
                                if tile_first:
                                    w = list(evp) + list(evs_v)
                                    tile_first = False
                                lhs = phik_s[pbuf][:, DTT * h + r0: DTT * h + r0 + rn]
                                rhs = vring_s[slot][:, 65 * h: 65 * h + 65]
                                ev_mm_all = K.mm(bank, lambda e, f, l, lhs=lhs, rhs=rhs,
                                                 rn=rn, bank=bank, hg=hg: e.matmul(
                                                     pb[bank][0:rn, 65 * hg: 65 * hg + 65],
                                                     lhs, rhs, start=f, stop=l),
                                                 first=first, last=last, waits=w)
                                opened[bank] = True
                    evac = []
                    for cdt, (r0, rn) in enumerate(DTC):
                        mo = cdt * HB + 65 * 4 * g
                        eva = K.act("vector", lambda e, cdt=cdt, rn=rn, mo=mo, b0=b0:
                                    e.tensor_add(
                                        master_s[0:rn, mo:mo + 260],
                                        master_s[0:rn, mo:mo + 260],
                                        pb[b0 + cdt][0:rn, 0:260]),
                                    waits=[ev_mm_all, ev_mz, ev_snap_prev])
                        use(f"pb{b0 + cdt}", eva)
                        evac.append(eva)
                    ev_ds_evac_prev = evac
                sidx = 2 * c + hc
                if sidx < 2 * NCH - 1:
                    ev_snap_prev = K.dma("gpsimd", snap_dram[sidx], master_s[:],
                                         waits=ev_ds_evac_prev, sem="mstg")
            ev_master_done = ev_ds_evac_prev
            for tabs in [4 * c + tl for tl in range(1, 5)]:
                use(f"phik{(tabs - 1) % 4}", ev_mm_all)
                use(f"vring{tabs % 5}", ev_mm_all)
            use(f"xfm{buf}", ev_mm_all)
            emit_tern(5)
            if c + 1 < NCH:
                ev_x_cur = load_xfm((c + 1) % 2, 4 * (c + 1), 5)

        emit_tern(len(tern_items))
        if full:
            ev_wtu = tref["wtu"]
            ev_wtd = tref["wtd"]
            if "wtu" in taps:
                K.tap_dma("tapw", taps["wtu"][:], wtu_dram[:], waits=[ev_wtu])
            if "wtd" in taps:
                K.tap_dma("tapw", taps["wtd"][:], wtd_dram[:], waits=[ev_wtd])
        ev_mo = K.dma("sync", master_out_d[:], master_s[:], waits=ev_master_done,
                      sem="msts")
        if full and fused_cc:
            ev_ci = K.dma("gpsimd", cc_in_d[:], master_s[:], waits=ev_master_done,
                          sem="mstg")

        def finalize():
            allw = [(s, v) for s, v in K.semn.items()
                    if s not in ("sV", "sC", "sT", "sGc", "sS")]
            K.emit("sync", lambda e: None, waits=allw)

        if not full:
            finalize()
            ctx_p1.close()
        else:
            ctx_p1.close()
            ev_p1 = [K.now("sT"), K.now("sV"), K.now("sC"), K.now("mstg"),
                     K.now("msts")] + \
                [K.now(f"vr{i}") for i in range(5)] + \
                [K.now("twr0"), K.now("twr1")]
            K.fence(("sync", "gpsimd", "vector", "scalar"), ev_p1)

            sstart_s = sb("sstart_s", [128, SB3])
            ctx_x = ExitStack()
            ag_s = [sb(f"ag{b}", [128, SB3], F32, ctx=ctx_x) for b in range(2)]
            if fused_cc:
                ev_cc = K.act("gpsimd", lambda e: e.collective_compute(
                    "AllGather", ALU.bypass,
                    replica_groups=[[0, 1, 2, 3], [4, 5, 6, 7]],
                    ins=[cc_in_d[:]], outs=[cc_out_d[:]]), waits=[ev_ci])
                peer_src = cc_out_d
                ev_peer_ready = [ev_cc]
            else:
                peer_src = peers_d
                ev_peer_ready = []
            ev_ss = K.act("vector", lambda e: e.memset(sstart_s[:], 0.0), waits=[ev_css])
            ev_add = {-2: None, -1: None}
            ev_g = {}

            def emit_agl(j):
                jb = j % 2
                ev_g[j] = K.dma("sync", ag_s[jb][:], peer_src[j],
                                waits=[ev_add[j - 2]] + ev_peer_ready, sem=f"agl{jb}")

            def emit_agadd(j):
                jb = j % 2
                ev_add[j] = K.act("vector", lambda e, j=j, jb=jb: e.scalar_tensor_tensor(
                    sstart_s[:], ag_s[jb][:], mskv_s[:, j:j + 1], sstart_s[:],
                    ALU.mult, ALU.add),
                    waits=[ev_g[j], ev_ss if j == 0 else ev_add[j - 1]])
            emit_agl(0)
            emit_agl(1)
            emit_agadd(0)
            emit_agl(2)
            emit_agadd(1)
            emit_agl(3)
            emit_agadd(2)
            emit_agadd(3)
            ev_ss = ev_add[3]
            ev_ss = K.act("vector", lambda e: e.tensor_scalar(
                sstart_s[:, 0:2 * HB], sstart_s[:, 0:2 * HB], 0.5, None, ALU.mult),
                waits=[ev_ss])
            if "sstart" in taps:
                K.tap_dma("taps2", taps["sstart"][:], sstart_s[:], waits=[ev_ss])
            ctx_x.close()
            K.fence(("sync", "gpsimd", "vector", "scalar"), [ev_ss])

            qkfm_s = [sb(f"qkfm{b}", [128, 4 * 640]) for b in range(2)]
            qkh_s = [sb(f"qkh{b}", [16, 4 * 640]) for b in range(2)]
            kkh_s = [sb(f"kkh{b}", [16, 4 * 640]) for b in range(2)]
            zrow_s = [sb(f"zrow{b}", [1, HB]) for b in range(2)]
            saug_s = [sb(f"saug{b}", [128, SB3]) for b in range(2)]
            phiqq_s = [sb(f"phiqq{b}", [128, 8 * 512]) for b in range(2)]
            qb_s = sb("qb_s", [128, 512])
            aet_s = [sb(f"aet{b}", [128, 2 * 512]) for b in range(2)]
            ehalo_s = [sb(f"ehalo{b}", [128, 512]) for b in range(2)]
            aprev_s = [sb(f"aprev{b}", [128, 512]) for b in range(2)]
            dens_s = sb("dens_s", [128, 2 * H], F32)
            rden_s = sb("rden_s", [128, 2 * H], F32)
            y_s = [sb(f"y{b}", [128, D], F32) for b in range(4)]
            xtm_s = [sb(f"xtm{b}", [128, D], F32) for b in range(2)]
            vstat_s = [sb(f"vstat{b}", [128, 16], F32) for b in range(2)]
            xnt_s = [sb(f"xnt{b}", [128, D]) for b in range(4)]
            xnf_s = [sb(f"xnf{b}", [128, 8 * 512]) for b in range(2)]
            hfm_s = sb("hfm_s", [128, 32 * 512])
            wstr_s = [sb(f"wstr{b}", [128, 1024]) for b in range(4)]
            wctr = [0]
            rmsi_s = sb("rmsi_s", [128, 8], F32)
            gsc_s = sb("gsc_s", [128, 128], F32)

            def ln_math(buf, ev_sum, ev_ssq, with_rms):
                vs = vstat_s[buf]
                e1 = K.act("vector", lambda e: e.tensor_scalar(
                    vs[:, 2:3], vs[:, 0:1], 1.0 / D, None, ALU.mult), waits=[ev_sum])
                e2 = K.act("vector", lambda e: e.tensor_scalar(
                    vs[:, 3:4], vs[:, 1:2], 1.0 / D, None, ALU.mult), waits=[ev_ssq])
                e3 = K.act("vector", lambda e: e.scalar_tensor_tensor(
                    vs[:, 4:5], vs[:, 2:3], vs[:, 2:3], vs[:, 3:4],
                    ALU.mult, ALU.subtract), waits=[e1, e2])
                e4 = K.act("vector", lambda e: e.tensor_scalar(
                    vs[:, 5:6], vs[:, 4:5], -1.0, 1e-5, ALU.mult, ALU.add), waits=[e3])
                e5 = K.act("scalar", lambda e: e.activation(
                    vs[:, 6:7], vs[:, 5:6], AF.Sqrt), waits=[e4])
                e6 = K.act("vector", lambda e: e.reciprocal(
                    vs[:, 7:8], vs[:, 6:7]), waits=[e5])
                if not with_rms:
                    e13 = K.act("vector", lambda e: e.tensor_copy(
                        vs[:, 13:14], vs[:, 7:8]), waits=[e6])
                else:
                    e8 = K.act("vector", lambda e: e.tensor_scalar(
                        vs[:, 8:9], vs[:, 4:5], -1.0, None, ALU.mult), waits=[e3])
                    e9 = K.act("scalar", lambda e: e.activation(
                        vs[:, 9:10], vs[:, 8:9], AF.Sqrt), waits=[e8])
                    e10 = K.act("vector", lambda e: e.tensor_scalar(
                        vs[:, 10:11], vs[:, 9:10], vs[:, 7:8], 1e-8, ALU.mult, ALU.add),
                        waits=[e9, e6])
                    e12 = K.act("vector", lambda e: e.reciprocal(
                        vs[:, 12:13], vs[:, 10:11]), waits=[e10])
                    e13 = K.act("vector", lambda e: e.tensor_scalar(
                        vs[:, 13:14], vs[:, 7:8], vs[:, 12:13], None, ALU.mult),
                        waits=[e12])
                e14 = K.act("vector", lambda e: e.tensor_scalar(
                    vs[:, 14:15], vs[:, 2:3], vs[:, 13:14], -1.0, ALU.mult, ALU.mult),
                    waits=[e13])
                return e13, e14

            ev_vload = {}
            _c_state = {}
            _p_state = {}

            def proj_stage(c):
                buf = c % 2
                ev_x = load_xfm(buf, 4 * c, 5)
                ev_qk_evac = []
                for f in range(4):
                    for k in range(8):
                        fi = (k == 0)
                        lhs = wqk_s[:, 512 * k + 128 * f: 512 * k + 128 * (f + 1)]
                        rhs_a = xfm_s[buf][:, 640 * k: 640 * k + 512]
                        rhs_b = xfm_s[buf][:, 640 * k + 512: 640 * k + 640]
                        w0 = (list(ev_x) + ev_qk_evac) if k == 0 else []
                        K.mm(0, lambda e, ff, ll, lhs=lhs, rhs_a=rhs_a: e.matmul(
                            pb[0][:, 0:512], lhs, rhs_a, start=ff, stop=ll),
                            first=fi, last=(k == 7), waits=w0)
                        K.mm(1, lambda e, ff, ll, lhs=lhs, rhs_b=rhs_b: e.matmul(
                            pb[1][:, 0:128], lhs, rhs_b, start=ff, stop=ll),
                            first=fi, last=(k == 7))
                    ev_mm = K.now("sT")
                    o1 = qkfm_s[buf][:, 640 * f: 640 * f + 512]
                    o2q = qkfm_s[buf][:, 640 * f + 512: 640 * f + 640]
                    e1 = K.act("vector", lambda e, f=f, o1=o1: e.tensor_scalar(
                        o1, pb[0][:, 0:512], bqk_s[:, f:f + 1], None, ALU.add),
                        waits=[ev_mm, ev_css] + (takefree(f"qkfm{buf}") if f == 0 else []))
                    e2 = K.act("scalar", lambda e, f=f, o2q=o2q: e.activation(
                        o2q, pb[1][:, 0:128],
                        AF.Identity, bias=bqk_s[:, f:f + 1], scale=1.0), waits=[ev_mm])
                    use("pb0", e1)
                    use("pb1", e2)
                    ev_qk_evac = [e1, e2]
                if "qkfm" in taps:
                    use(f"qkfm{buf}", K.tap_dma("tapq", taps["qkfm"][c], qkfm_s[buf][:],
                                                waits=ev_qk_evac))
                _p_state[c] = ev_qk_evac

            def attn_stage(c):
                buf = c % 2
                ld_tiles = ([0] if c == 0 else []) + [4 * c + tl for tl in range(1, 5)]
                for tabs in ld_tiles:
                    slot = tabs % 5
                    ev_vload[tabs] = K.dma("sync", vring_s[slot][:], vaug_dram[tabs],
                                           waits=takefree(f"vring{slot}"),
                                           sem=f"vr{slot}")
                ev_saug = {}
                ev_zrow = {}
                for hc in range(2):
                    sidx = 2 * c + hc - 1
                    wsa = takefree(f"saug{hc}") + [ev_ss]
                    if sidx < 0:
                        ev_s1 = K.act("vector", lambda e, hc=hc: e.tensor_copy(
                            saug_s[hc][:], sstart_s[:]), waits=wsa)
                    else:
                        ev_sl = K.dma("sync", saug_s[hc][:], snap_dram[sidx],
                                      waits=wsa, sem=f"sg{hc}")
                        ev_a = K.act("vector", lambda e, hc=hc: e.scalar_tensor_tensor(
                            saug_s[hc][:, 0:2 * HB], saug_s[hc][:, 0:2 * HB], 0.5,
                            sstart_s[:, 0:2 * HB], ALU.mult, ALU.add),
                            waits=[ev_sl])
                        ev_s1 = K.act("vector", lambda e, hc=hc: e.scalar_tensor_tensor(
                            saug_s[hc][:, 2 * HB:SB3], saug_s[hc][:, 2 * HB:SB3], 1.0,
                            sstart_s[:, 2 * HB:SB3], ALU.mult, ALU.add), waits=[ev_a])
                    ev_saug[hc] = ev_s1
                    ev_zrow[hc] = K.dma("sync", zrow_s[hc][:],
                                        saug_s[hc][16:17, 2 * HB:SB3],
                                        waits=[ev_s1] + takefree(f"zrow{hc}"),
                                        sem=f"zr{hc}")
                    use(f"saug{hc}", ev_zrow[hc])
                    if "saug" in taps:
                        use(f"saug{hc}", K.tap_dma("taps2", taps["saug"][2 * c + hc], saug_s[hc][:],
                                                   waits=[ev_s1]))

                ev_qk_evac = _p_state[c]

                ev_y_tile = {1: [], 2: [], 3: [], 4: []}
                ev_xt_tl = {}
                for tl in (1, 2):
                    tabs = 4 * c + tl
                    ev_xt_tl[tl] = K.dma("sync", xtm_s[tl - 1][:],
                                         xTM_d[128 * (tabs - 1):128 * tabs, :],
                                         waits=takefree(f"xtm{tl - 1}"),
                                         sem=f"xt{tl - 1}")

                def emit_rearr(g, buf=buf, ev_qk_evac=ev_qk_evac):
                    gpar = g % 2
                    r0 = 64 * (g % 2)
                    fbq = g // 2
                    fbk = 2 + g // 2
                    wqh = takefree(f"qkh{gpar}")
                    wkh = takefree(f"kkh{gpar}")
                    ev_kh = None
                    for mm_ in range(4):
                        K.dma("sync", qkh_s[gpar][:, 640 * mm_:640 * (mm_ + 1)],
                              qkfm_s[buf][r0 + 16 * mm_:r0 + 16 * (mm_ + 1),
                                          640 * fbq:640 * (fbq + 1)],
                              waits=(ev_qk_evac + wqh + wkh) if mm_ == 0 else (),
                              sem=f"qh{gpar}")
                        ev_kh = K.dma("sync", kkh_s[gpar][:, 640 * mm_:640 * (mm_ + 1)],
                                      qkfm_s[buf][r0 + 16 * mm_:r0 + 16 * (mm_ + 1),
                                                  640 * fbk:640 * (fbk + 1)],
                                      sem=f"qh{gpar}")
                    use(f"qkfm{buf}", ev_kh)
                    return [ev_kh]

                ev_qkh_g = {0: emit_rearr(0)}
                for g in range(4):
                    gpar = g % 2
                    ev_qkh = ev_qkh_g[g]
                    wphi = takefree(f"phiqq{gpar}")
                    ev_prod_all = []
                    for hg in range(4):
                        qrow = qkh_s[gpar][:, 640 * hg + 128: 640 * hg + 640]
                        wq0 = ev_qkh + wphi + [ev_ones]
                        evA0 = K.mm(2, lambda e, f, l, qrow=qrow: e.matmul(
                            pb[2][:, 0:512], repm_s[0:16, 0:128], qrow,
                            start=f, stop=l), first=True, last=True, waits=wq0)
                        evA1 = K.mm(3, lambda e, f, l, qrow=qrow: e.matmul(
                            pb[3][:, 0:512], repm_s[0:16, 128:256], qrow,
                            start=f, stop=l), first=True, last=True)
                        evB = K.mm(4, lambda e, f, l, qrow=qrow: e.matmul(
                            pb[4][:, 0:512], repm_s[0:16, 256:384], qrow,
                            start=f, stop=l), first=True, last=True)
                        evqb = K.act("scalar", lambda e: e.copy(
                            qb_s[:], pb[4][:, 0:512]), waits=[evB] + takefree("qb"))
                        use("pb4", evqb)
                        ev0 = K.act("vector", lambda e, hg=hg, gpar=gpar: e.tensor_mul(
                            phiqq_s[gpar][:, 512 * (2 * hg): 512 * (2 * hg + 1)],
                            pb[2][:, 0:512], qb_s[:]), waits=[evA0, evqb])
                        ev1 = K.act("vector", lambda e, hg=hg, gpar=gpar: e.tensor_mul(
                            phiqq_s[gpar][:, 512 * (2 * hg + 1): 512 * (2 * hg + 2)],
                            pb[3][:, 0:512], qb_s[:]), waits=[evA1, evqb])
                        use("pb2", ev0)
                        use("pb3", ev1)
                        use("qb", ev0, ev1)
                        ev_prod_all += [ev0, ev1]
                    if g + 1 < 4:
                        ev_qkh_g[g + 1] = emit_rearr(g + 1)
                    if "phiqq" in taps:
                        use(f"phiqq{gpar}", K.tap_dma("tapq", taps["phiqq"][c, g],
                                                      phiqq_s[gpar][:],
                                                      waits=ev_prod_all))
                    SCB = {1: (5, 6), 2: (2, 3), 3: (5, 6), 4: (2, 3)}
                    YB = {1: (0, 1), 2: (4, 7), 3: (0, 1), 4: (4, 7)}
                    st_ae = {}

                    def emit_scores(tl, g=g, gpar=gpar):
                        b5, b6 = SCB[tl]
                        op5 = op6 = False
                        for hg in range(4):
                            kcur = kkh_s[gpar][:, 640 * hg + 128 * tl: 640 * hg + 128 * (tl + 1)]
                            kprv = kkh_s[gpar][:, 640 * hg + 128 * (tl - 1): 640 * hg + 128 * tl]
                            qcur = qkh_s[gpar][:, 640 * hg + 128 * tl: 640 * hg + 128 * (tl + 1)]
                            K.mm(b5, lambda e, f, l, kcur=kcur, qcur=qcur, hg=hg, b5=b5:
                                 e.matmul(pb[b5][:, 128 * hg:128 * (hg + 1)], kcur, qcur,
                                          start=f, stop=l), first=not op5, last=(hg == 3),
                                 waits=ev_qkh if not op5 else ())
                            op5 = True
                            K.mm(b6, lambda e, f, l, kprv=kprv, qcur=qcur, hg=hg, b6=b6:
                                 e.matmul(pb[b6][:, 128 * hg:128 * (hg + 1)], kprv, qcur,
                                          start=f, stop=l), first=not op6, last=(hg == 3))
                            op6 = True
                        return K.now("sT")

                    def emit_ae(tl, ev_sc, g=g):
                        tabs = 4 * c + tl
                        b5, b6 = SCB[tl]
                        has_aprev = ((tl - 1) % 2 == 1)
                        apar = (4 * g + tl) % 2
                        waet = takefree(f"aet{apar}") + takefree(f"ehalo{apar}") + \
                            takefree(f"aprev{apar}")
                        evA = K.act("scalar", lambda e, apar=apar, b5=b5: e.activation(
                            aet_s[apar][:, 0:512], pb[b5][:, 0:512], AF.Square,
                            bias=1.0, scale=1.0), waits=[ev_sc] + waet)
                        evE = K.act("scalar", lambda e, apar=apar, b5=b5: e.activation(
                            aet_s[apar][:, 512:1024], pb[b5][:, 0:512], AF.Exp),
                            waits=[ev_sc])
                        evEh = K.act("scalar", lambda e, apar=apar, b6=b6: e.activation(
                            ehalo_s[apar][:], pb[b6][:, 0:512], AF.Exp),
                            waits=[ev_sc])
                        use(f"pb{b5}", evA, evE)
                        use(f"pb{b6}", evEh)
                        evAp2 = None
                        if has_aprev:
                            evAp = K.act("scalar", lambda e, apar=apar, b6=b6: e.activation(
                                aprev_s[apar][:], pb[b6][:, 0:512], AF.Square,
                                bias=1.0, scale=1.0), waits=[ev_sc])
                            use(f"pb{b6}", evAp)
                            evAp2 = K.act("vector", lambda e, apar=apar: e.tensor_scalar(
                                aprev_s[apar][:], aprev_s[apar][:],
                                0.5, 0.5, ALU.mult, ALU.add), waits=[evAp])
                        mA = mcA_s[:].unsqueeze(1).broadcast_to([128, 4, 128])
                        mC = mbC_s[:].unsqueeze(1).broadcast_to([128, 4, 128])
                        mh_t = mbH0_s if tabs == 1 else mbH_s
                        mH = mh_t[:].unsqueeze(1).broadcast_to([128, 4, 128])
                        evA2 = K.act("vector", lambda e, apar=apar: e.tensor_scalar(
                            aet_s[apar][:, 0:512], aet_s[apar][:, 0:512],
                            0.5, 0.5, ALU.mult, ALU.add), waits=[evA])
                        evA3 = K.act("vector", lambda e, apar=apar, mA=mA: e.tensor_mul(
                            aet_s[apar][:, 0:512].rearrange("p (g q) -> p g q", g=4),
                            aet_s[apar][:, 0:512].rearrange("p (g q) -> p g q", g=4),
                            mA), waits=[evA2])
                        evE2 = K.act("vector", lambda e, apar=apar, mC=mC: e.tensor_mul(
                            aet_s[apar][:, 512:1024].rearrange("p (g q) -> p g q", g=4),
                            aet_s[apar][:, 512:1024].rearrange("p (g q) -> p g q", g=4),
                            mC), waits=[evE])
                        evEh2 = K.act("vector", lambda e, apar=apar, mH=mH: e.tensor_mul(
                            ehalo_s[apar][:].rearrange("p (g q) -> p g q", g=4),
                            ehalo_s[apar][:].rearrange("p (g q) -> p g q", g=4),
                            mH), waits=[evEh])
                        return (apar, evA3, evE2, evEh2, evAp2, has_aprev)

                    def emit_v(tl, st, g=g, gpar=gpar):
                        tabs = 4 * c + tl
                        slot = tabs % 5
                        pslot = (tabs - 1) % 5
                        hcpar = (tl - 1) // 2
                        apar, evA3, evE2, evEh2, evAp2, has_aprev = st
                        b0, b1 = YB[tl]
                        op0 = op1 = False
                        for hg in range(4):
                            h = 4 * g + hg
                            qcur = qkh_s[gpar][:, 640 * hg + 128 * tl: 640 * hg + 128 * (tl + 1)]
                            K.mm(b0, lambda e, f, l, apar=apar, hg=hg, slot=slot, h=h, b0=b0:
                                 e.matmul(pb[b0][:, 65 * hg:65 * hg + 65],
                                          aet_s[apar][:, 128 * hg:128 * (hg + 1)],
                                          vring_s[slot][:, 65 * h:65 * h + 65],
                                          start=f, stop=l),
                                 first=not op0, last=False,
                                 waits=([ev_saug[hcpar], evA3, ev_vload.get(tabs)])
                                 if not op0 else ())
                            op0 = True
                            if has_aprev:
                                K.mm(b0, lambda e, f, l, apar=apar, hg=hg, pslot=pslot, h=h, b0=b0:
                                     e.matmul(pb[b0][:, 65 * hg:65 * hg + 65],
                                              aprev_s[apar][:, 128 * hg:128 * (hg + 1)],
                                              vring_s[pslot][:, 65 * h:65 * h + 65],
                                              start=f, stop=l),
                                     first=False, last=False,
                                     waits=[evAp2, ev_vload.get(tabs - 1)] if hg == 0 else ())
                            K.mm(b0, lambda e, f, l, h=h, hg=hg, hcpar=hcpar, b0=b0: e.matmul(
                                pb[b0][:, 65 * hg:65 * hg + 65], ones_row[:],
                                zrow_s[hcpar][0:1, 65 * h: 65 * h + 65],
                                start=f, stop=l), first=False, last=False,
                                waits=[ev_zrow[hcpar]] if hg == 0 else ())
                            K.mm(b0, lambda e, f, l, h=h, hg=hg, qcur=qcur, hcpar=hcpar, b0=b0:
                                 e.matmul(
                                     pb[b0][:, 65 * hg:65 * hg + 65], qcur,
                                     saug_s[hcpar][0:16, 2 * HB + 65 * h: 2 * HB + 65 * h + 65],
                                     start=f, stop=l), first=False, last=False)
                            K.mm(b0, lambda e, f, l, h=h, hg=hg, gpar=gpar, tl=tl, hcpar=hcpar, b0=b0:
                                 e.matmul(
                                     pb[b0][:, 65 * hg:65 * hg + 65],
                                     phiqq_s[gpar][:, 512 * (2 * hg) + 128 * (tl - 1):
                                                   512 * (2 * hg) + 128 * tl],
                                     saug_s[hcpar][:, 0 * HB + 65 * h: 0 * HB + 65 * h + 65],
                                     start=f, stop=l), first=False, last=False,
                                 waits=ev_prod_all if hg == 0 else ())
                            K.mm(b0, lambda e, f, l, h=h, hg=hg, gpar=gpar, tl=tl, hcpar=hcpar, b0=b0:
                                 e.matmul(
                                     pb[b0][:, 65 * hg:65 * hg + 65],
                                     phiqq_s[gpar][:, 512 * (2 * hg + 1) + 128 * (tl - 1):
                                                   512 * (2 * hg + 1) + 128 * tl],
                                     saug_s[hcpar][:, 1 * HB + 65 * h: 1 * HB + 65 * h + 65],
                                     start=f, stop=l), first=False, last=(hg == 3))
                            K.mm(b1, lambda e, f, l, apar=apar, hg=hg, slot=slot, h=h, b1=b1:
                                 e.matmul(pb[b1][:, 65 * hg:65 * hg + 65],
                                          aet_s[apar][:, 512 + 128 * hg:512 + 128 * (hg + 1)],
                                          vring_s[slot][:, 65 * h:65 * h + 65],
                                          start=f, stop=l),
                                 first=not op1, last=False,
                                 waits=[evE2] if not op1 else ())
                            op1 = True
                            K.mm(b1, lambda e, f, l, apar=apar, hg=hg, pslot=pslot, h=h, b1=b1:
                                 e.matmul(pb[b1][:, 65 * hg:65 * hg + 65],
                                          ehalo_s[apar][:, 128 * hg:128 * (hg + 1)],
                                          vring_s[pslot][:, 65 * h:65 * h + 65],
                                          start=f, stop=l),
                                 first=False, last=(hg == 3),
                                 waits=[evEh2, ev_vload.get(tabs - 1)] if hg == 0 else ())
                        ev_vi = K.now("sT")
                        use(f"aet{apar}", ev_vi)
                        use(f"ehalo{apar}", ev_vi)
                        use(f"aprev{apar}", ev_vi)
                        use(f"vring{pslot}", ev_vi)
                        return ev_vi

                    def emit_evac(tl, ev_vi, g=g):
                        b0, b1 = YB[tl]
                        wd = takefree(f"dens{tl % 2}")
                        dlin = bass.AP(pb[b0], 64, [[512, 128], [65, 4]])
                        dloc = bass.AP(pb[b1], 64, [[512, 128], [65, 4]])
                        dc = 8 * (tl % 2)
                        ed1 = K.act("vector", lambda e, dlin=dlin, dc=dc: e.tensor_scalar(
                            dens_s[:, dc:dc + 4], dlin, 1e-8, None, ALU.add),
                            waits=[ev_vi] + wd)
                        ed2 = K.act("vector", lambda e, dloc=dloc, dc=dc: e.tensor_copy(
                            dens_s[:, 16 + dc:16 + dc + 4], dloc), waits=[ev_vi])
                        er1 = K.act("vector", lambda e, dc=dc: e.reciprocal(
                            rden_s[:, dc:dc + 4], dens_s[:, dc:dc + 4]), waits=[ed1])
                        er2 = K.act("vector", lambda e, dc=dc: e.reciprocal(
                            rden_s[:, 16 + dc:16 + dc + 4], dens_s[:, 16 + dc:16 + dc + 4]),
                            waits=[ed2])
                        use(f"dens{tl % 2}", er1, er2)
                        use(f"pb{b0}", ed1)
                        use(f"pb{b1}", ed2)
                        wyb = takefree(f"y{tl - 1}") if g == 0 else []
                        for hg in range(4):
                            h = 4 * g + hg
                            e_sc = K.act("scalar", lambda e, h=h, hg=hg, tl=tl, b0=b0, dc=dc:
                                         e.activation(
                                             y_s[tl - 1][:, 64 * h:64 * h + 64],
                                             pb[b0][:, 65 * hg:65 * hg + 64], AF.Identity,
                                             scale=rden_s[:, dc + hg:dc + hg + 1]),
                                         waits=[er1] + wyb)
                            e_lo = K.act("vector", lambda e, h=h, hg=hg, tl=tl, b1=b1, dc=dc:
                                         e.scalar_tensor_tensor(
                                             y_s[tl - 1][:, 64 * h:64 * h + 64],
                                             pb[b1][:, 65 * hg:65 * hg + 64],
                                             rden_s[:, 16 + dc + hg:16 + dc + hg + 1],
                                             y_s[tl - 1][:, 64 * h:64 * h + 64],
                                             ALU.mult, ALU.add), waits=[er2, e_sc])
                            use(f"pb{b0}", e_sc)
                            use(f"pb{b1}", e_lo)
                            ev_y_tile[tl].append(e_lo)

                    ev_sc_t = {1: emit_scores(1)}
                    ev_sc_t[2] = emit_scores(2)
                    st_ae[1] = emit_ae(1, ev_sc_t[1])
                    ev_v1 = emit_v(1, st_ae[1])
                    ev_sc_t[3] = emit_scores(3)
                    st_ae[2] = emit_ae(2, ev_sc_t[2])
                    ev_v2 = emit_v(2, st_ae[2])
                    emit_evac(1, ev_v1)
                    ev_sc_t[4] = emit_scores(4)
                    st_ae[3] = emit_ae(3, ev_sc_t[3])
                    ev_v3 = emit_v(3, st_ae[3])
                    emit_evac(2, ev_v2)
                    st_ae[4] = emit_ae(4, ev_sc_t[4])
                    ev_v4 = emit_v(4, st_ae[4])
                    emit_evac(3, ev_v3)
                    emit_evac(4, ev_v4)
                    use(f"qkh{gpar}", K.now("sT"))
                    use(f"kkh{gpar}", K.now("sT"))
                    use(f"phiqq{gpar}", K.now("sT"))

                ev_xn_tl = {}
                for tl in range(1, 5):
                    tabs = 4 * c + tl
                    tpar = tl % 2
                    if tl in ev_xt_tl:
                        ev_xt = ev_xt_tl[tl]
                    else:
                        ev_xt = K.dma("sync", xtm_s[(tl - 1) % 2][:],
                                      xTM_d[128 * (tabs - 1):128 * tabs, :],
                                      waits=takefree(f"xtm{(tl - 1) % 2}"),
                                      sem=f"xt{(tl - 1) % 2}")
                    ev_ytap = None
                    if "y" in taps:
                        ev_ytap = K.tap_dma("tapy", taps["y"][128 * (tabs - 1):128 * tabs, :],
                                            y_s[tl - 1][:], waits=ev_y_tile[tl])
                    ev_r1 = K.act("vector", lambda e, tl=tl: e.tensor_add(
                        y_s[tl - 1][:], y_s[tl - 1][:], xtm_s[(tl - 1) % 2][:]),
                        waits=ev_y_tile[tl] + [ev_xt, ev_ytap])
                    use(f"xtm{(tl - 1) % 2}", ev_r1)
                    ev_r1d = K.dma("gpsimd", r1_dram[128 * (tabs - 1):128 * tabs, :],
                                   y_s[tl - 1][:], waits=[ev_r1], sem=f"ylg{tl - 1}")
                    if "r1" in taps:
                        use(f"y{tl - 1}",
                            K.tap_dma("tapy", taps["r1"][128 * (tabs - 1):128 * tabs, :],
                                      y_s[tl - 1][:], waits=[ev_r1]))
                    ev_sum = K.act("vector", lambda e, tl=tl: e.reduce_sum(
                        vstat_s[tl % 2][:, 0:1], y_s[tl - 1][:], AX.X),
                        waits=[ev_r1] + takefree(f"vstat{tl % 2}"))
                    wxn = takefree(f"xnt{tl - 1}")
                    ev_ssq = K.act("scalar", lambda e, tl=tl: e.activation(
                        xnt_s[tl - 1][:], y_s[tl - 1][:], AF.Square,
                        accum_out=vstat_s[tl % 2][:, 1:2]), waits=[ev_r1] + wxn)
                    e_scale, e_bias = ln_math(tl % 2, ev_sum, ev_ssq, with_rms=True)
                    ev_xn = K.act("scalar", lambda e, tl=tl: e.activation(
                        xnt_s[tl - 1][:], y_s[tl - 1][:], AF.Identity,
                        scale=vstat_s[tl % 2][:, 13:14], bias=vstat_s[tl % 2][:, 14:15]),
                        waits=[e_scale, e_bias, ev_ssq])
                    ev_xn_tl[tl] = ev_xn
                    use(f"y{tl - 1}", ev_xn, ev_r1d, ev_sum)
                    use(f"vstat{tl % 2}", ev_xn)
                    if "xn" in taps:
                        use(f"xnt{tl - 1}",
                            K.tap_dma("tapx", taps["xn"][128 * (tabs - 1):128 * tabs, :],
                                      xnt_s[tl - 1][:], waits=[ev_xn]))

                _c_state[c] = dict(buf=buf, ev_xnf_parts=None,
                                   ld_tiles=ld_tiles, ev_xn_tl=ev_xn_tl)

            def emit_trans(c):
                ev_xn_tl = _c_state[c]['ev_xn_tl']
                ev_xnf_parts = []
                wtr = takefree(f"xnf{c % 2}")
                for tl in range(1, 5):
                    pbt = pb[2][:, 0:512].bitcast(BF16)
                    for k in range(8):
                        K.trans(2, lambda e, tl=tl, k=k, pbt=pbt: e.transpose(
                            pbt[:, 128 * k:128 * (k + 1)],
                            xnt_s[tl - 1][:, 128 * k:128 * (k + 1)], idenb_s[:]),
                            waits=([ev_xn_tl[tl]] + wtr) if k == 0 else ())
                    ev_tr = K.now("sT")
                    o2 = xnf_s[c % 2][:].rearrange("p (k t) -> p k t", k=8)[:, :, 128 * (tl - 1):128 * tl]
                    i2 = pbt.rearrange("p (k t) -> p k t", k=8)
                    ev1 = K.act("vector", lambda e, o2=o2, i2=i2: e.tensor_copy(o2, i2),
                                waits=[ev_tr])
                    use("pb2", ev1)
                    ev_xnf_parts += [ev1]
                    use(f"xnt{tl - 1}", ev1)
                if "xnf" in taps:
                    use(f"xnf{c % 2}", K.tap_dma("tapx", taps["xnf"][c], xnf_s[c % 2][:],
                                         waits=ev_xnf_parts))
                _c_state[c]['ev_xnf_parts'] = ev_xnf_parts

            def ffn_stage(c):
                st = _c_state[c]
                buf = st['buf']
                ev_xn_tl = st['ev_xn_tl']
                ld_tiles = st['ld_tiles']
                ev_xnf_parts = st['ev_xnf_parts']
                ev_gelu_all = []
                for f in range(32):
                    wb = wctr[0] % 4
                    wctr[0] += 1
                    ev_wl = K.dma("sync", wstr_s[wb][:],
                                  wtu_dram[:, 1024 * f:1024 * (f + 1)],
                                  waits=takefree(f"wstr{wb}") +
                                  ([ev_wtu] if c == 0 and f < 4 else []), sem=f"ws{wb}")
                    bank = 4 + f % 2
                    for k in range(8):
                        K.mm(bank, lambda e, ff, ll, k=k, wb=wb, bank=bank: e.matmul(
                            pb[bank][:, 0:512],
                            wstr_s[wb][:, 128 * k:128 * (k + 1)],
                            xnf_s[c % 2][:, 512 * k:512 * (k + 1)], start=ff, stop=ll),
                            first=(k == 0), last=(k == 7),
                            waits=(ev_xnf_parts + [ev_wl]) if k == 0 else ())
                    ev_up = K.now("sT")
                    use(f"wstr{wb}", ev_up)
                    ev_g = K.act("scalar", lambda e, f=f, bank=bank: e.activation(
                        hfm_s[:, 512 * f:512 * (f + 1)], pb[bank][:, 0:512], AF.Gelu),
                        waits=[ev_up] + (takefree("hfm") if f == 0 else []))
                    use(f"pb{bank}", ev_g)
                    ev_gelu_all.append(ev_g)
                ev_gelu = ev_gelu_all[-1]
                if "hf" in taps:
                    use("hfm", K.tap_dma("tapx", taps["hf"][c], hfm_s[:],
                                         waits=[ev_gelu]))
                ev_rmsh = {}
                for tl in range(1, 5):
                    tpar = tl % 2
                    for f in range(32):
                        hsl = hfm_s[:, 512 * f + 128 * (tl - 1): 512 * f + 128 * tl]
                        K.mm(6, lambda e, ff, ll, hsl=hsl: e.matmul(
                            pb[6][:, 0:128], hsl, hsl, start=ff, stop=ll),
                            first=(f == 0), last=(f == 31),
                            waits=[ev_gelu] if f == 0 else ())
                    ev_gr = K.now("sT")
                    wvs = takefree(f"vstat{tpar}")
                    ev_dm = K.act("vector", lambda e: e.tensor_mul(
                        gsc_s[:], pb[6][:, 0:128], idmask_s[:]),
                        waits=[ev_gr, ev_css] + takefree("gsc"))
                    use("pb6", ev_dm)
                    ev_dg = K.act("vector", lambda e, tpar=tpar: e.reduce_sum(
                        vstat_s[tpar][:, 1:2], gsc_s[:], AX.X),
                        waits=[ev_dm] + wvs)
                    use("gsc", ev_dg)
                    ev_s = K.act("scalar", lambda e, tpar=tpar: e.activation(
                        vstat_s[tpar][:, 2:3], vstat_s[tpar][:, 1:2], AF.Sqrt),
                        waits=[ev_dg])
                    ev_m = K.act("vector", lambda e, tpar=tpar: e.tensor_scalar(
                        vstat_s[tpar][:, 3:4], vstat_s[tpar][:, 2:3], 1.0 / 64.0, 1e-8,
                        ALU.mult, ALU.add), waits=[ev_s])
                    ev_rmsh[tl] = K.act("vector", lambda e, tpar=tpar, tl=tl: e.reciprocal(
                        rmsi_s[:, tl - 1:tl], vstat_s[tpar][:, 3:4]),
                        waits=[ev_m] + takefree(f"rmsi{tl - 1}"))
                    use(f"vstat{tpar}", ev_rmsh[tl])
                for p2 in range(2):
                    tls = [2 * p2 + 1, 2 * p2 + 2]
                    banks = {tls[0]: (0, 1), tls[1]: (2, 3)}
                    for f in range(32):
                        wb = wctr[0] % 4
                        wctr[0] += 1
                        ev_wl = K.dma("sync", wstr_s[wb][:],
                                      wtd_dram[:, 1024 * f:1024 * (f + 1)],
                                      waits=takefree(f"wstr{wb}") +
                                      ([ev_wtd] if c == 0 and p2 == 0 and f < 4 else []),
                                      sem=f"ws{wb}")
                        for tl in tls:
                            b0, b1 = banks[tl]
                            hsl = hfm_s[:, 512 * f + 128 * (tl - 1): 512 * f + 128 * tl]
                            K.mm(b0, lambda e, ff, ll, hsl=hsl, wb=wb, b0=b0: e.matmul(
                                pb[b0][:, 0:512], hsl, wstr_s[wb][:, 0:512],
                                start=ff, stop=ll), first=(f == 0), last=(f == 31),
                                waits=([ev_wl, ev_gelu]) if f == 0 else [ev_wl])
                            K.mm(b1, lambda e, ff, ll, hsl=hsl, wb=wb, b1=b1: e.matmul(
                                pb[b1][:, 0:512], hsl, wstr_s[wb][:, 512:1024],
                                start=ff, stop=ll), first=(f == 0), last=(f == 31))
                        use(f"wstr{wb}", K.now("sT"))
                    ev_dn_all = K.now("sT")
                    er2_t = {}
                    for tl in tls:
                        tabs = 4 * c + tl
                        b0, b1 = banks[tl]
                        r2b = y_s[tl - 1]
                        ev_r1l = K.dma("sync", r2b[:],
                                       r1_dram[128 * (tabs - 1):128 * tabs, :],
                                       waits=takefree(f"y{tl - 1}"), sem=f"yl{tl - 1}")
                        er2a = K.act("vector", lambda e, b0=b0, tl=tl, r2b=r2b:
                                     e.scalar_tensor_tensor(
                                         r2b[:, 0:512], pb[b0][:, 0:512],
                                         rmsi_s[:, tl - 1:tl], r2b[:, 0:512],
                                         ALU.mult, ALU.add),
                                     waits=[ev_dn_all, ev_r1l, ev_rmsh[tl]])
                        er2b = K.act("vector", lambda e, b1=b1, tl=tl, r2b=r2b:
                                     e.scalar_tensor_tensor(
                                         r2b[:, 512:1024], pb[b1][:, 0:512],
                                         rmsi_s[:, tl - 1:tl], r2b[:, 512:1024],
                                         ALU.mult, ALU.add), waits=[ev_dn_all, ev_r1l])
                        use(f"rmsi{tl - 1}", er2a, er2b)
                        use(f"pb{b0}", er2a)
                        use(f"pb{b1}", er2b)
                        er2_t[tl] = (er2a, er2b)
                    for tl in tls:
                        tabs = 4 * c + tl
                        tpar = tl % 2
                        r2b = y_s[tl - 1]
                        er2a, er2b = er2_t[tl]
                        wvs = takefree(f"vstat{tpar}")
                        ev_sum2 = K.act("vector", lambda e, tpar=tpar, r2b=r2b: e.reduce_sum(
                            vstat_s[tpar][:, 0:1], r2b[:], AX.X),
                            waits=[er2a, er2b] + wvs)
                        wout = takefree(f"xtm{(tl - 1) % 2}")
                        ev_ssq2 = K.act("scalar", lambda e, tpar=tpar, tl=tl, r2b=r2b: e.activation(
                            xtm_s[(tl - 1) % 2][:], r2b[:], AF.Square,
                            accum_out=vstat_s[tpar][:, 1:2]),
                            waits=[er2a, er2b] + wout)
                        e_sc2, e_bi2 = ln_math(tpar, ev_sum2, ev_ssq2, with_rms=False)
                        ev_o = K.act("scalar", lambda e, tpar=tpar, tl=tl, r2b=r2b: e.activation(
                            xtm_s[(tl - 1) % 2][:], r2b[:], AF.Identity,
                            scale=vstat_s[tpar][:, 13:14], bias=vstat_s[tpar][:, 14:15]),
                            waits=[e_sc2, e_bi2, ev_ssq2])
                        use(f"y{tl - 1}", ev_o)
                        use(f"vstat{tpar}", ev_o)
                        ev_od = K.dma("sync", out_d[128 * (tabs - 1):128 * tabs, :],
                                      xtm_s[(tl - 1) % 2][:], waits=[ev_o],
                                      sem=f"xt{(tl - 1) % 2}")
                        use(f"xtm{(tl - 1) % 2}", ev_od)
                use("hfm", ev_dn_all)
                use(f"xnf{c % 2}", K.now("sT"))
                use(f"xfm{buf}", K.now("sT"))
                for hc_ in range(2):
                    use(f"saug{hc_}", K.now("sT"))
                    use(f"zrow{hc_}", K.now("sT"))
                use(f"qkfm{buf}", K.now("sT"))

            for c in range(NCH):
                proj_stage(c)
                attn_stage(c)
                if c >= 1:
                    ffn_stage(c - 1)
                emit_trans(c)
            ffn_stage(NCH - 1)
            finalize()

        block = ctx.enter_context(nc.Block())
        for eng_name in ENGS:
            def body(eng, _ops=K.ops[eng_name]):
                for fn in _ops:
                    fn(eng)
            getattr(block, eng_name)(body)

    return nc, taps


TCORE = 4096


def _wka(Wk):
    wka = np.zeros((1024, 272), np.float32)
    wt = Wk.T
    for h in range(16):
        wka[:, 17 * h:17 * h + 16] = wt[:, 16 * h:16 * (h + 1)]
    return wka


def _bkra(bk):
    b = np.zeros((1, 272), np.float32)
    for h in range(16):
        b[0, 17 * h:17 * h + 16] = bk[16 * h:16 * (h + 1)]
        b[0, 17 * h + 16] = 1.0
    return b


def _host_inputs(x_main, x_halo, Wq, bq, Wk, bk, Wv, bv, w_up, w_down, pos, full):
    x_main = np.ascontiguousarray(x_main, dtype=np.float32)
    if x_halo is None:
        x_halo = np.zeros((64, D), np.float32)
    xT_cols = np.concatenate([np.zeros((64, D), np.float32),
                              np.asarray(x_halo, np.float32), x_main], axis=0)
    ins = {
        "xT": np.ascontiguousarray(xT_cols.T),
        "wqk": np.ascontiguousarray(np.concatenate([Wq.T, Wk.T], axis=1)),
        "bkra": _bkra(bk),
        "wka": _wka(Wk),
    }
    wva = np.zeros((D, HB), np.float32)
    bvar = np.zeros((1, HB), np.float32)
    for h in range(H):
        wva[:, 65 * h:65 * h + 64] = Wv.T[:, 64 * h:64 * h + 64]
        bvar[0, 65 * h:65 * h + 64] = bv[64 * h:64 * h + 64]
        bvar[0, 65 * h + 64] = 1.0
    ins["wva"] = wva
    ins["bvar"] = bvar
    if not full:
        return ins
    ins["xTM"] = x_main
    ins["bqk"] = np.concatenate([bq, bk]).reshape(512, 1).astype(np.float32)
    ins["wup_sw"] = np.ascontiguousarray(
        w_up.T.reshape(8, 128, 32, 128).transpose(1, 2, 0, 3).reshape(128, 8 * DFF))
    ins["wdn_sw"] = np.ascontiguousarray(
        w_down.T.reshape(32, 128, 1024).transpose(1, 0, 2).reshape(128, 32 * 1024))
    kk, qq = np.meshgrid(np.arange(128), np.arange(128), indexing="ij")
    ins["mcA"] = (kk <= qq).astype(np.float32)
    ins["mbC"] = ((kk <= qq) & (kk >= qq - 63)).astype(np.float32)
    mbHx = (kk >= qq + 65).astype(np.float32)
    ins["mbH"] = mbHx
    ins["mbH0"] = mbHx if pos > 0 else np.zeros((128, 128), np.float32)
    msk = np.zeros((128, 4), np.float32)
    msk[:, :pos] = 1.0
    ins["mskv"] = msk
    ins["iden"] = np.eye(128, dtype=np.float32)
    rep = np.zeros((16, 3 * 128), np.float32)
    for p in range(128):
        if p // 16 < 8:
            rep[p // 16, p] = 1.0
        rep[8 + p // 16, 128 + p] = 1.0
        rep[p % 16, 256 + p] = 1.0
    ins["repm"] = rep
    return ins


def kernel(L0_attn_x, L0_attn_Wq, L0_attn_bq, L0_attn_Wk, L0_attn_bk,
           L0_attn_Wv, L0_attn_bv, L0_ffn_w_up, L0_ffn_w_down):
    x = np.ascontiguousarray(np.asarray(L0_attn_x, dtype=np.float32))
    Wq = np.asarray(L0_attn_Wq, np.float32)
    bq = np.asarray(L0_attn_bq, np.float32)
    Wk = np.asarray(L0_attn_Wk, np.float32)
    bk = np.asarray(L0_attn_bk, np.float32)
    Wv = np.asarray(L0_attn_Wv, np.float32)
    bv = np.asarray(L0_attn_bv, np.float32)
    w_up = np.asarray(L0_ffn_w_up, np.float32)
    w_down = np.asarray(L0_ffn_w_down, np.float32)
    B, N, _ = x.shape
    assert (B, N) == (2, 16384)

    ins_full = []
    for c in range(8):
        b, p = c // 4, c % 4
        xm = x[b, TCORE * p:TCORE * (p + 1)]
        xh = None if p == 0 else x[b, TCORE * p - 64:TCORE * p]
        ins_full.append(_host_inputs(xm, xh, Wq, bq, Wk, bk, Wv, bv,
                                     w_up, w_down, p, True))

    nc2, _ = build_program(T=TCORE, mode="full", fused_cc=True)
    r2 = run_bass_kernel_spmd(nc2, ins_full, core_ids=list(range(8)))

    out = np.empty((2, 16384, 1024), np.float32)
    for c in range(8):
        b, p = c // 4, c % 4
        out[b, TCORE * p:TCORE * (p + 1)] = np.asarray(r2.results[c]["out"])
    return out


# version 25
# speedup vs baseline: 1.0374x; 1.0374x over previous
import math
import numpy as np
import ml_dtypes
from contextlib import ExitStack

import concourse.bass as bass
import concourse.mybir as mybir
from concourse.bass_utils import run_bass_kernel_spmd

F32 = mybir.dt.float32
BF16 = mybir.dt.bfloat16
AF = mybir.ActivationFunctionType
ALU = mybir.AluOpType
AX = mybir.AxisListType
BF = ml_dtypes.bfloat16

H, DP, DH, DTT, D, DFF = 16, 16, 64, 273, 1024, 4096
HB = H * 65
SB3 = 3 * HB
ENGS = ("sync", "gpsimd", "vector", "scalar", "tensor")
DTC = [(0, 128), (128, 128), (256, 17)]


class KB:
    def __init__(self, nc, ctx):
        self.nc = nc
        self.ctx = ctx
        self.ops = {e: [] for e in ENGS}
        self.semn = {}
        self.sems = {}
        self.pb_open = [False] * 8
        self.freeb = {}
        for s in ("sV", "sC", "sT", "sGc", "sS"):
            self.sems[s] = ctx.enter_context(nc.semaphore(s))

    def use(self, name, *evs):
        self.freeb.setdefault(name, []).extend(e for e in evs if e is not None)

    def takefree(self, name):
        evs = self.freeb.get(name, [])
        self.freeb[name] = []
        return evs

    def sb(self, name, shape, dt=BF16, ctx=None):
        return (ctx or self.ctx).enter_context(self.nc.sbuf_tensor(name, shape, dt))

    def bump(self, s, amt):
        self.semn[s] = self.semn.get(s, 0) + amt
        return (s, self.semn[s])

    def now(self, s):
        return (s, self.semn.get(s, 0))

    def getsem(self, name):
        if name not in self.sems:
            self.sems[name] = self.ctx.enter_context(self.nc.semaphore(name))
        return self.sems[name]

    def emit(self, engine, fn, waits=(), raw=False):
        seen = {}
        for w in waits:
            if w is None:
                continue
            s, v = w
            if v > seen.get(s, -1):
                seen[s] = v
        out = list(seen.items())
        for s, _v in out:
            self.getsem(s)
        sems = self.sems
        def op(eng, fn=fn, waits=tuple(out)):
            for (s, v) in waits:
                if v > 0:
                    eng.wait_ge(sems[s], v)
            fn(eng)
        self.ops[engine].append(op)

    def fence(self, engines, waits):
        for e in engines:
            self.emit(e, lambda eng: None, waits=waits)

    def dma(self, q, out, in_, waits=(), sem=None):
        assert sem is not None, "dma needs a stream sem name"
        self.getsem(sem)
        ev = self.bump(sem, 16)
        sems = self.sems
        def fn(eng, out=out, in_=in_, sem=sem):
            eng.dma_start(out=out, in_=in_).then_inc(sems[sem], 16)
        self.emit(q, fn, waits)
        return ev

    def act(self, engine, build, waits=(), amt=1):
        sem = {"vector": "sV", "scalar": "sC", "tensor": "sT",
               "gpsimd": "sGc", "sync": "sS"}[engine]
        ev = self.bump(sem, amt)
        sems = self.sems
        def fn(eng, build=build, sem=sem, amt=amt):
            build(eng).then_inc(sems[sem], amt)
        self.emit(engine, fn, waits)
        return ev

    def mm(self, bank, build, *, first, last, waits=()):
        if first:
            assert not self.pb_open[bank], f"bank {bank} already open"
            self.pb_open[bank] = True
            waits = list(waits) + self.takefree(f"pb{bank}")
        else:
            assert self.pb_open[bank], f"bank {bank} not open"
        if last:
            self.pb_open[bank] = False
        return self.act("tensor",
                        lambda e, b=build, f=first, l=last: b(e, f, l), waits)

    def tap_dma(self, stream, out, in_, waits=()):
        if not hasattr(self, "_last_tap"):
            self._last_tap = {}
        prev = self._last_tap.get(stream)
        ev = self.dma("gpsimd", out, in_, waits=list(waits) + [prev], sem=stream)
        self._last_tap[stream] = ev
        return ev

    def trans(self, bank, build, waits=()):
        waits = list(waits) + self.takefree(f"pb{bank}")
        return self.act("tensor", build, waits)


def build_program(T=512, mode="full", fused_cc=False, debug_taps=()):
    assert T % 512 == 0
    NT = T // 128 + 1
    NCH = T // 512
    nc = bass.Bass("TRN2", target_bir_lowering=False)
    full = (mode == "full")

    xT_d = nc.dram_tensor("xT", [D, NT * 128], F32, kind="ExternalInput")
    wqk_d = nc.dram_tensor("wqk", [D, 512], F32, kind="ExternalInput")
    wva_d = nc.dram_tensor("wva", [D, HB], F32, kind="ExternalInput")
    wka_d = nc.dram_tensor("wka", [D, 272], F32, kind="ExternalInput")
    bkra_d = nc.dram_tensor("bkra", [1, 272], F32, kind="ExternalInput")
    bvar_d = nc.dram_tensor("bvar", [1, HB], F32, kind="ExternalInput")
    master_out_d = nc.dram_tensor("master_out", [128, SB3], F32, kind="ExternalOutput")
    vaug_dram = nc.dram_tensor("vaug_scr", [NT, 128, HB], BF16)
    snap_dram = nc.dram_tensor("snap_scr", [max(2 * NCH - 1, 1), 128, SB3], BF16)
    if full:
        xTM_d = nc.dram_tensor("xTM", [T, D], F32, kind="ExternalInput")
        bqk_d = nc.dram_tensor("bqk", [512, 1], F32, kind="ExternalInput")
        wup_d = nc.dram_tensor("wup_sw", [128, 8 * DFF], F32, kind="ExternalInput")
        wdn_d = nc.dram_tensor("wdn_sw", [128, 32 * D], F32, kind="ExternalInput")
        mcA_d = nc.dram_tensor("mcA", [128, 128], F32, kind="ExternalInput")
        mbC_d = nc.dram_tensor("mbC", [128, 128], F32, kind="ExternalInput")
        mbH_d = nc.dram_tensor("mbH", [128, 128], F32, kind="ExternalInput")
        mbH0_d = nc.dram_tensor("mbH0", [128, 128], F32, kind="ExternalInput")
        mskv_d = nc.dram_tensor("mskv", [128, 4], F32, kind="ExternalInput")
        iden_d = nc.dram_tensor("iden", [128, 128], F32, kind="ExternalInput")
        repm_d = nc.dram_tensor("repm", [16, 3 * 128], F32, kind="ExternalInput")
        out_d = nc.dram_tensor("out", [T, D], F32, kind="ExternalOutput")
        r1_dram = nc.dram_tensor("r1_scr", [T, D], F32)
        wtu_dram = nc.dram_tensor("wtu_scr", [128, 8 * DFF], BF16)
        wtd_dram = nc.dram_tensor("wtd_scr", [128, 32 * D], BF16)
        if fused_cc:
            cc_in_d = nc.dram_tensor("cc_in", [128, SB3], F32)
            cc_out_d = nc.dram_tensor("cc_out", [4, 128, SB3], F32)
        else:
            peers_d = nc.dram_tensor("peers", [4, 128, SB3], F32, kind="ExternalInput")

    taps = {}
    def mktap(name, shape):
        if name in debug_taps:
            taps[name] = nc.dram_tensor("tap_" + name, shape, F32, kind="ExternalOutput")
    mktap("vaug", [NT, 128, HB])
    mktap("phik", [NT - 1, 128, H * DTT])
    if full:
        mktap("qkfm", [NCH, 128, 4 * 640])
        mktap("phiqq", [NCH, 4, 128, 8 * 512])
        mktap("saug", [2 * NCH, 128, SB3])
        mktap("sstart", [128, SB3])
        mktap("y", [T, D])
        mktap("r1", [T, D])
        mktap("xn", [T, D])
        mktap("xnf", [NCH, 128, 8 * 512])
        mktap("hf", [NCH, 128, 32 * 512])
        mktap("wtu", [128, 8 * DFF])
        mktap("wtd", [128, 32 * D])

    ctx = ExitStack()
    with ctx:
        K = KB(nc, ctx)
        sb = K.sb
        use = K.use
        takefree = K.takefree
        freeb = K.freeb

        wqk_s = sb("wqk_s", [128, 8 * 512])
        ones_row = sb("ones_r", [1, 128])
        allon_s = sb("allon_s", [128, 128], F32)
        xfm_s = [sb(f"xfm{b}", [128, 8 * 640]) for b in range(2)]
        vring_s = [sb(f"vring{b}", [128, HB]) for b in range(5)]
        pb = [ctx.enter_context(nc.psum_tensor(f"pb{i}", [128, 512], F32)) for i in range(8)]

        for k in range(8):
            K.dma("gpsimd", wqk_s[:, 512 * k:512 * (k + 1)], wqk_d[128 * k:128 * (k + 1), :],
                  sem="cst")
        ev_ones = K.act("vector", lambda e: e.memset(ones_row[:], 1.0))
        ev_allon = K.act("vector", lambda e: e.memset(allon_s[:], 1.0))
        if full:
            mcA_s = sb("mcA_s", [128, 128])
            mbC_s = sb("mbC_s", [128, 128])
            mbH_s = sb("mbH_s", [128, 128])
            mbH0_s = sb("mbH0_s", [128, 128])
            mskv_s = sb("mskv_s", [128, 4], F32)
            idenb_s = sb("idenb_s", [128, 128])
            idmask_s = sb("idmask_s", [128, 128], F32)
            repm_s = sb("repm_s", [16, 3 * 128])
            bqk_s = sb("bqk_s", [128, 4], F32)
            K.dma("gpsimd", mcA_s[:], mcA_d[:], sem="cst")
            K.dma("gpsimd", mbC_s[:], mbC_d[:], sem="cst")
            K.dma("gpsimd", mbH_s[:], mbH_d[:], sem="cst")
            K.dma("gpsimd", mbH0_s[:], mbH0_d[:], sem="cst")
            K.dma("gpsimd", idenb_s[:], iden_d[:], sem="cst")
            K.dma("gpsimd", repm_s[:], repm_d[:], sem="cst")
            K.dma("sync", idmask_s[:], iden_d[:], sem="css")
            K.dma("sync", mskv_s[:], mskv_d[:], sem="css")
            for f in range(4):
                K.dma("sync", bqk_s[:, f:f + 1], bqk_d[128 * f:128 * (f + 1), :], sem="css")
        ev_cst = K.now("cst")
        ev_css = K.now("css")

        ctx_p1 = ExitStack()
        tern_items = []
        tref = {"wtu": None, "wtd": None}
        if full:
            tld_s = [sb(f"tld{b}", [128, 2048], F32, ctx=ctx_p1) for b in range(2)]
            tsg_s = sb("tsg_s", [128, 2048], F32, ctx=ctx_p1)
            tsgb_s = sb("tsgb_s", [128, 2048], ctx=ctx_p1)
            tsnb_s = sb("tsnb_s", [128, 2048], ctx=ctx_p1)
            twt_s = [sb(f"twt{b}", [128, 2048], ctx=ctx_p1) for b in range(2)]
            tstat_s = sb("tstat_s", [128, 4], F32, ctx=ctx_p1)
            talpha_s = sb("talpha_s", [128, 2], F32, ctx=ctx_p1)

            def mk_tern_items(w_src, wt_dst, ncols, acol, key):
                nch = ncols // 2048
                st = {}

                def alpha_start():
                    st["prev"] = K.act("vector", lambda e: e.memset(tstat_s[:, 0:1], 0.0))
                tern_items.append(alpha_start)

                def alpha_batch(i):
                    b = i % 2
                    evl = K.dma("sync", tld_s[b][:], w_src[:, 2048 * i:2048 * (i + 1)],
                                waits=takefree(f"tld{b}"), sem=f"tld{b}")
                    evr = K.act("scalar", lambda e, b=b: e.activation(
                        tsg_s[:], tld_s[b][:], AF.Abs,
                        accum_out=tstat_s[:, 1:2]),
                        waits=[evl, st["prev"]] + takefree("tsg"))
                    use(f"tld{b}", evr)
                    use("tsg", evr)
                    st["prev"] = K.act("vector", lambda e: e.tensor_add(
                        tstat_s[:, 0:1], tstat_s[:, 0:1], tstat_s[:, 1:2]), waits=[evr])
                for i in range(nch):
                    tern_items.append(lambda i=i: alpha_batch(i))

                def alpha_fin():
                    evmm = K.mm(7, lambda e, f, l: e.matmul(
                        pb[7][:, 0:1], allon_s[:], tstat_s[:, 0:1], start=f, stop=l),
                        first=True, last=True, waits=[st["prev"], ev_allon])
                    st["th"] = K.act("vector", lambda e: e.tensor_scalar(
                        talpha_s[:, acol:acol + 1], pb[7][:, 0:1],
                        0.5 / (128.0 * ncols), None, ALU.mult), waits=[evmm])
                    use("pb7", st["th"])
                tern_items.append(alpha_fin)

                def tern_batch(i):
                    b = i % 2
                    evl = K.dma("sync", tld_s[b][:], w_src[:, 2048 * i:2048 * (i + 1)],
                                waits=takefree(f"tld{b}"), sem=f"tld{b}")
                    evab = K.act("scalar", lambda e, b=b: e.activation(
                        tsg_s[:], tld_s[b][:], AF.Abs), waits=[evl] + takefree("tsg"))
                    evg = K.act("vector", lambda e: e.tensor_scalar(
                        tsgb_s[:], tsg_s[:], talpha_s[:, acol:acol + 1], None, ALU.is_gt),
                        waits=[evab, st["th"]] + takefree("tsgb"))
                    use("tsg", evg)
                    evsg = K.act("scalar", lambda e, b=b: e.sign(tsnb_s[:], tld_s[b][:]),
                                 waits=[evab] + takefree("tsnb"))
                    use(f"tld{b}", evsg)
                    evm = K.act("vector", lambda e, b=b: e.tensor_mul(
                        twt_s[b][:], tsnb_s[:], tsgb_s[:]),
                        waits=[evsg, evg] + takefree(f"twt{b}"))
                    use("tsgb", evm)
                    use("tsnb", evm)
                    evd = K.dma("gpsimd", wt_dst[:, 2048 * i:2048 * (i + 1)], twt_s[b][:],
                                waits=[evm], sem=f"twr{b}")
                    use(f"twt{b}", evd)
                    tref[key] = evd
                for i in range(nch):
                    tern_items.append(lambda i=i: tern_batch(i))

            mk_tern_items(wup_d, wtu_dram, 8 * DFF, 0, "wtu")
            mk_tern_items(wdn_d, wtd_dram, 32 * D, 1, "wtd")

        def emit_tern(n):
            for _ in range(n):
                if tern_items:
                    tern_items.pop(0)()

        ktm_s = [sb(f"ktm{b}", [128, 272], ctx=ctx_p1) for b in range(2)]
        wka_s = sb("wka_s", [128, 8 * 272], ctx=ctx_p1)
        phik_s = [sb(f"phik{b}", [128, H * DTT], ctx=ctx_p1) for b in range(4)]
        master_s = sb("master_s", [128, SB3], F32, ctx=ctx_p1)
        wva_s = sb("wva_s", [128, 8 * HB], ctx=ctx_p1)
        bkra_s = sb("bkra_s", [1, 272], ctx=ctx_p1)
        bvar_s = sb("bvar_s", [1, HB], ctx=ctx_p1)
        for k in range(8):
            K.dma("gpsimd", wva_s[:, HB * k:HB * (k + 1)], wva_d[128 * k:128 * (k + 1), :],
                  sem="cp1")
        for k in range(8):
            K.dma("gpsimd", wka_s[:, 272 * k:272 * (k + 1)],
                  wka_d[128 * k:128 * (k + 1), :], sem="cp1")
        K.dma("gpsimd", bkra_s[:], bkra_d[:], sem="cp1")
        ev_bvar = K.dma("gpsimd", bvar_s[:], bvar_d[:], sem="cp1")

        ev_mz = K.act("vector", lambda e: e.memset(master_s[:], 0.0))
        ev_phi1 = []
        for b in range(4):
            ap = phik_s[b][:].rearrange("p (h r) -> p h r", h=H)[:, :, 272:273]
            ev_phi1.append(K.act("vector", lambda e, ap=ap: e.memset(ap, 1.0)))

        def load_xfm(buf, t0, ntl):
            w = takefree(f"xfm{buf}")
            for k in range(8):
                ev = K.dma("gpsimd",
                           xfm_s[buf][:, 640 * k:640 * k + 128 * ntl],
                           xT_d[128 * k:128 * (k + 1), 128 * t0:128 * (t0 + ntl)],
                           waits=w if k == 0 else (), sem=f"xf{buf}")
            return [ev]

        def proj_kv_tile(buf, tl, tabs, ev_x):
            par = proj_kv_tile.count % 2
            proj_kv_tile.count += 1
            slot = tabs % 5
            wk_v = takefree(f"vring{slot}")
            wk_k = takefree(f"ktm{par}")
            for k in range(8):
                fi = (k == 0)
                lhs = xfm_s[buf][:, 640 * k + 128 * tl: 640 * k + 128 * (tl + 1)]
                w0 = (list(ev_x) + [ev_cst, ev_css, K.now("cp1")] if k == 0 else [])
                K.mm(0, lambda e, f, l, k=k, lhs=lhs: e.matmul(
                    pb[0][:, 0:272], lhs,
                    wka_s[:, 272 * k: 272 * (k + 1)], start=f, stop=l),
                    first=fi, last=False, waits=w0)
                K.mm(1, lambda e, f, l, k=k, lhs=lhs: e.matmul(
                    pb[1][:, 0:512], lhs,
                    wva_s[:, HB * k: HB * k + 512], start=f, stop=l),
                    first=fi, last=False)
                K.mm(2, lambda e, f, l, k=k, lhs=lhs: e.matmul(
                    pb[2][:, 0:512], lhs,
                    wva_s[:, HB * k + 512: HB * k + 1024], start=f, stop=l),
                    first=fi, last=False)
                K.mm(3, lambda e, f, l, k=k, lhs=lhs: e.matmul(
                    pb[3][:, 0:16], lhs,
                    wva_s[:, HB * k + 1024: HB * k + 1040], start=f, stop=l),
                    first=fi, last=False)
            ev0 = K.mm(0, lambda e, f, l: e.matmul(
                pb[0][:, 0:272], ones_row[:], bkra_s[:], start=f, stop=l),
                first=False, last=True, waits=[ev_ones])
            ev1 = K.mm(1, lambda e, f, l: e.matmul(
                pb[1][:, 0:512], ones_row[:], bvar_s[:, 0:512], start=f, stop=l),
                first=False, last=True)
            ev2 = K.mm(2, lambda e, f, l: e.matmul(
                pb[2][:, 0:512], ones_row[:], bvar_s[:, 512:1024], start=f, stop=l),
                first=False, last=True)
            ev3 = K.mm(3, lambda e, f, l: e.matmul(
                pb[3][:, 0:16], ones_row[:], bvar_s[:, 1024:1040], start=f, stop=l),
                first=False, last=True)
            evk = K.act("scalar", lambda e: e.activation(
                ktm_s[par][:], pb[0][:, 0:272], AF.Identity), waits=[ev0] + wk_k)
            evva = K.act("scalar", lambda e: e.activation(
                vring_s[slot][:, 0:512], pb[1][:, 0:512], AF.Identity),
                waits=[ev1] + wk_v)
            evvb = K.act("scalar", lambda e: e.activation(
                vring_s[slot][:, 512:1024], pb[2][:, 0:512], AF.Identity), waits=[ev2])
            evvc = K.act("scalar", lambda e: e.activation(
                vring_s[slot][:, 1024:1040], pb[3][:, 0:16], AF.Identity), waits=[ev3])
            evs_v = [evva, evvb, evvc]
            use("pb0", evk)
            use("pb1", evva)
            use("pb2", evvb)
            use("pb3", evvc)
            evd = K.dma("sync", vaug_dram[tabs], vring_s[slot][:], waits=evs_v,
                        sem=f"vr{slot}")
            use(f"vring{slot}", evd)
            if "vaug" in taps:
                use(f"vring{slot}", K.tap_dma("tapv", taps["vaug"][tabs],
                                              vring_s[slot][:], waits=[evd]))
            return par, slot, evk, evs_v
        proj_kv_tile.count = 0

        def build_phik(pbuf, par_ktm, ev_k, tabs):
            wk = takefree(f"phik{pbuf}") + [ev_phi1[pbuf]]
            kt = ktm_s[par_ktm]
            ph = phik_s[pbuf][:].rearrange("p (h r) -> p h r", h=H)
            outer_all = ph[:, :, 0:272].rearrange("p h (i j) -> p h i j", i=17)
            kh_i = kt[:].rearrange("p (h i) -> p h i", i=17)
            ev_o = []
            for h0, h1, eng in ((0, 16, "vector"),):
                nh = h1 - h0
                oo = outer_all[:, h0:h1]
                ii = kh_i[:, h0:h1].unsqueeze(3).broadcast_to([128, nh, 17, 16])
                jj = kh_i[:, h0:h1, 0:16].unsqueeze(2).broadcast_to([128, nh, 17, 16])
                ev_o.append(K.act(eng, lambda e, oo=oo, ii=ii, jj=jj: e.tensor_tensor(
                    oo, ii, jj, ALU.mult), waits=[ev_k] + wk))
            ev_kc = None
            use(f"ktm{par_ktm}", *ev_o)
            if "phik" in taps:
                use(f"phik{pbuf}", K.tap_dma("tapk", taps["phik"][tabs - 1],
                                             phik_s[pbuf][:], waits=ev_o))
            return ev_o

        ev_x_cur = load_xfm(0, 0, 5)
        ev_master_done = None
        ev_ds_evac_prev = []
        ev_snap_prev = None
        for c in range(NCH):
            buf = c % 2
            ev_x = ev_x_cur
            tiles = ([0] if c == 0 else []) + [4 * c + tl for tl in range(1, 5)]
            phik_evs = {}
            vslot = {}
            for tabs in tiles:
                tl = tabs - 4 * c
                par, slot, evk, evs_v = proj_kv_tile(buf, tl, tabs, ev_x)
                vslot[tabs] = (slot, evs_v)
                if tabs > 0:
                    pbuf = (tabs - 1) % 4
                    phik_evs[tabs] = (pbuf, build_phik(pbuf, par, evk, tabs))
            emit_tern(5)
            ev_mm_all = None
            for hc in range(2):
                tiles_half = [4 * c + 2 * hc + tl for tl in range(1, 3)]
                for g in range(4):
                    gg = 4 * hc + g
                    b0 = 4 if gg % 2 == 0 else 0
                    opened = {b0: False, b0 + 1: False, b0 + 2: False}
                    for tabs in tiles_half:
                        pbuf, evp = phik_evs[tabs]
                        slot, evs_v = vslot[tabs]
                        tile_first = True
                        for hg in range(4):
                            h = 4 * g + hg
                            for cdt, (r0, rn) in enumerate(DTC):
                                bank = b0 + cdt
                                first = not opened[bank]
                                last = (tabs == tiles_half[-1]) and (hg == 3)
                                w = []
                                if tile_first:
                                    w = list(evp) + list(evs_v)
                                    tile_first = False
                                lhs = phik_s[pbuf][:, DTT * h + r0: DTT * h + r0 + rn]
                                rhs = vring_s[slot][:, 65 * h: 65 * h + 65]
                                ev_mm_all = K.mm(bank, lambda e, f, l, lhs=lhs, rhs=rhs,
                                                 rn=rn, bank=bank, hg=hg: e.matmul(
                                                     pb[bank][0:rn, 65 * hg: 65 * hg + 65],
                                                     lhs, rhs, start=f, stop=l),
                                                 first=first, last=last, waits=w)
                                opened[bank] = True
                    evac = []
                    for cdt, (r0, rn) in enumerate(DTC):
                        mo = cdt * HB + 65 * 4 * g
                        eva = K.act("vector", lambda e, cdt=cdt, rn=rn, mo=mo, b0=b0:
                                    e.tensor_add(
                                        master_s[0:rn, mo:mo + 260],
                                        master_s[0:rn, mo:mo + 260],
                                        pb[b0 + cdt][0:rn, 0:260]),
                                    waits=[ev_mm_all, ev_mz, ev_snap_prev])
                        use(f"pb{b0 + cdt}", eva)
                        evac.append(eva)
                    ev_ds_evac_prev = evac
                sidx = 2 * c + hc
                if sidx < 2 * NCH - 1:
                    ev_snap_prev = K.dma("gpsimd", snap_dram[sidx], master_s[:],
                                         waits=ev_ds_evac_prev, sem="mstg")
            ev_master_done = ev_ds_evac_prev
            for tabs in [4 * c + tl for tl in range(1, 5)]:
                use(f"phik{(tabs - 1) % 4}", ev_mm_all)
                use(f"vring{tabs % 5}", ev_mm_all)
            use(f"xfm{buf}", ev_mm_all)
            emit_tern(5)
            if c + 1 < NCH:
                ev_x_cur = load_xfm((c + 1) % 2, 4 * (c + 1), 5)

        emit_tern(len(tern_items))
        if full:
            ev_wtu = tref["wtu"]
            ev_wtd = tref["wtd"]
            if "wtu" in taps:
                K.tap_dma("tapw", taps["wtu"][:], wtu_dram[:], waits=[ev_wtu])
            if "wtd" in taps:
                K.tap_dma("tapw", taps["wtd"][:], wtd_dram[:], waits=[ev_wtd])
        ev_mo = K.dma("sync", master_out_d[:], master_s[:], waits=ev_master_done,
                      sem="msts")
        if full and fused_cc:
            ev_ci = K.dma("gpsimd", cc_in_d[:], master_s[:], waits=ev_master_done,
                          sem="mstg")

        def finalize():
            allw = [(s, v) for s, v in K.semn.items()
                    if s not in ("sV", "sC", "sT", "sGc", "sS")]
            K.emit("sync", lambda e: None, waits=allw)

        if not full:
            finalize()
            ctx_p1.close()
        else:
            ctx_p1.close()
            ev_p1 = [K.now("sT"), K.now("sV"), K.now("sC"), K.now("mstg"),
                     K.now("msts")] + \
                [K.now(f"vr{i}") for i in range(5)] + \
                [K.now("twr0"), K.now("twr1")]
            K.fence(("sync", "gpsimd", "vector", "scalar"), ev_p1)

            sstart_s = sb("sstart_s", [128, SB3])
            ctx_x = ExitStack()
            ag_s = [sb(f"ag{b}", [128, SB3], F32, ctx=ctx_x) for b in range(2)]
            if fused_cc:
                ev_cc = K.act("gpsimd", lambda e: e.collective_compute(
                    "AllGather", ALU.bypass,
                    replica_groups=[[0, 1, 2, 3], [4, 5, 6, 7]],
                    ins=[cc_in_d[:]], outs=[cc_out_d[:]]), waits=[ev_ci])
                peer_src = cc_out_d
                ev_peer_ready = [ev_cc]
            else:
                peer_src = peers_d
                ev_peer_ready = []
            ev_ss = K.act("vector", lambda e: e.memset(sstart_s[:], 0.0), waits=[ev_css])
            ev_add = {-2: None, -1: None}
            ev_g = {}

            def emit_agl(j):
                jb = j % 2
                ev_g[j] = K.dma("sync", ag_s[jb][:], peer_src[j],
                                waits=[ev_add[j - 2]] + ev_peer_ready, sem=f"agl{jb}")

            def emit_agadd(j):
                jb = j % 2
                ev_add[j] = K.act("vector", lambda e, j=j, jb=jb: e.scalar_tensor_tensor(
                    sstart_s[:], ag_s[jb][:], mskv_s[:, j:j + 1], sstart_s[:],
                    ALU.mult, ALU.add),
                    waits=[ev_g[j], ev_ss if j == 0 else ev_add[j - 1]])
            emit_agl(0)
            emit_agl(1)
            emit_agadd(0)
            emit_agl(2)
            emit_agadd(1)
            emit_agl(3)
            emit_agadd(2)
            emit_agadd(3)
            ev_ss = ev_add[3]
            ev_ss = K.act("vector", lambda e: e.tensor_scalar(
                sstart_s[:, 0:2 * HB], sstart_s[:, 0:2 * HB], 0.5, None, ALU.mult),
                waits=[ev_ss])
            if "sstart" in taps:
                K.tap_dma("taps2", taps["sstart"][:], sstart_s[:], waits=[ev_ss])
            ctx_x.close()
            K.fence(("sync", "gpsimd", "vector", "scalar"), [ev_ss])

            qkfm_s = [sb(f"qkfm{b}", [128, 4 * 640]) for b in range(2)]
            qkh_s = [sb(f"qkh{b}", [16, 4 * 640]) for b in range(2)]
            kkh_s = [sb(f"kkh{b}", [16, 4 * 640]) for b in range(2)]
            zrow_s = [sb(f"zrow{b}", [1, HB]) for b in range(2)]
            saug_s = [sb(f"saug{b}", [128, SB3]) for b in range(2)]
            phiqq_s = [sb(f"phiqq{b}", [128, 8 * 512]) for b in range(2)]
            qb_s = sb("qb_s", [128, 512])
            aet_s = [sb(f"aet{b}", [128, 2 * 512]) for b in range(2)]
            ehalo_s = [sb(f"ehalo{b}", [128, 512]) for b in range(2)]
            aprev_s = [sb(f"aprev{b}", [128, 512]) for b in range(2)]
            dens_s = sb("dens_s", [128, 2 * H], F32)
            rden_s = sb("rden_s", [128, 2 * H], F32)
            y_s = [sb(f"y{b}", [128, D], F32) for b in range(4)]
            xtm_s = [sb(f"xtm{b}", [128, D], F32) for b in range(2)]
            vstat_s = [sb(f"vstat{b}", [128, 16], F32) for b in range(2)]
            xnt_s = [sb(f"xnt{b}", [128, D]) for b in range(4)]
            xnf_s = [sb(f"xnf{b}", [128, 8 * 512]) for b in range(2)]
            hfm_s = sb("hfm_s", [128, 32 * 512])
            wstr_s = [sb(f"wstr{b}", [128, 1024]) for b in range(4)]
            wctr = [0]
            rmsi_s = sb("rmsi_s", [128, 8], F32)
            gsc_s = sb("gsc_s", [128, 128], F32)

            def ln_math(buf, ev_sum, ev_ssq, with_rms):
                vs = vstat_s[buf]
                e1 = K.act("vector", lambda e: e.tensor_scalar(
                    vs[:, 2:3], vs[:, 0:1], 1.0 / D, None, ALU.mult), waits=[ev_sum])
                e2 = K.act("vector", lambda e: e.tensor_scalar(
                    vs[:, 3:4], vs[:, 1:2], 1.0 / D, None, ALU.mult), waits=[ev_ssq])
                e3 = K.act("vector", lambda e: e.scalar_tensor_tensor(
                    vs[:, 4:5], vs[:, 2:3], vs[:, 2:3], vs[:, 3:4],
                    ALU.mult, ALU.subtract), waits=[e1, e2])
                e4 = K.act("vector", lambda e: e.tensor_scalar(
                    vs[:, 5:6], vs[:, 4:5], -1.0, 1e-5, ALU.mult, ALU.add), waits=[e3])
                e5 = K.act("scalar", lambda e: e.activation(
                    vs[:, 6:7], vs[:, 5:6], AF.Sqrt), waits=[e4])
                e6 = K.act("vector", lambda e: e.reciprocal(
                    vs[:, 7:8], vs[:, 6:7]), waits=[e5])
                if not with_rms:
                    e13 = K.act("vector", lambda e: e.tensor_copy(
                        vs[:, 13:14], vs[:, 7:8]), waits=[e6])
                else:
                    e8 = K.act("vector", lambda e: e.tensor_scalar(
                        vs[:, 8:9], vs[:, 4:5], -1.0, None, ALU.mult), waits=[e3])
                    e9 = K.act("scalar", lambda e: e.activation(
                        vs[:, 9:10], vs[:, 8:9], AF.Sqrt), waits=[e8])
                    e10 = K.act("vector", lambda e: e.tensor_scalar(
                        vs[:, 10:11], vs[:, 9:10], vs[:, 7:8], 1e-8, ALU.mult, ALU.add),
                        waits=[e9, e6])
                    e12 = K.act("vector", lambda e: e.reciprocal(
                        vs[:, 12:13], vs[:, 10:11]), waits=[e10])
                    e13 = K.act("vector", lambda e: e.tensor_scalar(
                        vs[:, 13:14], vs[:, 7:8], vs[:, 12:13], None, ALU.mult),
                        waits=[e12])
                e14 = K.act("vector", lambda e: e.tensor_scalar(
                    vs[:, 14:15], vs[:, 2:3], vs[:, 13:14], -1.0, ALU.mult, ALU.mult),
                    waits=[e13])
                return e13, e14

            ev_vload = {}
            _c_state = {}
            _p_state = {}

            def proj_stage(c):
                buf = c % 2
                ev_x = load_xfm(buf, 4 * c, 5)
                ev_qk_evac = []
                for f in range(4):
                    for k in range(8):
                        fi = (k == 0)
                        lhs = wqk_s[:, 512 * k + 128 * f: 512 * k + 128 * (f + 1)]
                        rhs_a = xfm_s[buf][:, 640 * k: 640 * k + 512]
                        rhs_b = xfm_s[buf][:, 640 * k + 512: 640 * k + 640]
                        w0 = (list(ev_x) + ev_qk_evac) if k == 0 else []
                        K.mm(0, lambda e, ff, ll, lhs=lhs, rhs_a=rhs_a: e.matmul(
                            pb[0][:, 0:512], lhs, rhs_a, start=ff, stop=ll),
                            first=fi, last=(k == 7), waits=w0)
                        K.mm(1, lambda e, ff, ll, lhs=lhs, rhs_b=rhs_b: e.matmul(
                            pb[1][:, 0:128], lhs, rhs_b, start=ff, stop=ll),
                            first=fi, last=(k == 7))
                    ev_mm = K.now("sT")
                    o1 = qkfm_s[buf][:, 640 * f: 640 * f + 512]
                    o2q = qkfm_s[buf][:, 640 * f + 512: 640 * f + 640]
                    e1 = K.act("vector", lambda e, f=f, o1=o1: e.tensor_scalar(
                        o1, pb[0][:, 0:512], bqk_s[:, f:f + 1], None, ALU.add),
                        waits=[ev_mm, ev_css] + (takefree(f"qkfm{buf}") if f == 0 else []))
                    e2 = K.act("scalar", lambda e, f=f, o2q=o2q: e.activation(
                        o2q, pb[1][:, 0:128],
                        AF.Identity, bias=bqk_s[:, f:f + 1], scale=1.0), waits=[ev_mm])
                    use("pb0", e1)
                    use("pb1", e2)
                    ev_qk_evac = [e1, e2]
                if "qkfm" in taps:
                    use(f"qkfm{buf}", K.tap_dma("tapq", taps["qkfm"][c], qkfm_s[buf][:],
                                                waits=ev_qk_evac))
                _p_state[c] = ev_qk_evac

            def attn_stage(c):
                buf = c % 2
                ld_tiles = ([0] if c == 0 else []) + [4 * c + tl for tl in range(1, 5)]
                for tabs in ld_tiles:
                    slot = tabs % 5
                    ev_vload[tabs] = K.dma("sync", vring_s[slot][:], vaug_dram[tabs],
                                           waits=takefree(f"vring{slot}"),
                                           sem=f"vr{slot}")
                ev_saug = {}
                ev_zrow = {}
                for hc in range(2):
                    sidx = 2 * c + hc - 1
                    wsa = takefree(f"saug{hc}") + [ev_ss]
                    if sidx < 0:
                        ev_s1 = K.act("vector", lambda e, hc=hc: e.tensor_copy(
                            saug_s[hc][:], sstart_s[:]), waits=wsa)
                    else:
                        ev_sl = K.dma("sync", saug_s[hc][:], snap_dram[sidx],
                                      waits=wsa, sem=f"sg{hc}")
                        ev_a = K.act("vector", lambda e, hc=hc: e.scalar_tensor_tensor(
                            saug_s[hc][:, 0:2 * HB], saug_s[hc][:, 0:2 * HB], 0.5,
                            sstart_s[:, 0:2 * HB], ALU.mult, ALU.add),
                            waits=[ev_sl])
                        ev_s1 = K.act("vector", lambda e, hc=hc: e.scalar_tensor_tensor(
                            saug_s[hc][:, 2 * HB:SB3], saug_s[hc][:, 2 * HB:SB3], 1.0,
                            sstart_s[:, 2 * HB:SB3], ALU.mult, ALU.add), waits=[ev_a])
                    ev_saug[hc] = ev_s1
                    ev_zrow[hc] = K.dma("sync", zrow_s[hc][:],
                                        saug_s[hc][16:17, 2 * HB:SB3],
                                        waits=[ev_s1] + takefree(f"zrow{hc}"),
                                        sem=f"zr{hc}")
                    use(f"saug{hc}", ev_zrow[hc])
                    if "saug" in taps:
                        use(f"saug{hc}", K.tap_dma("taps2", taps["saug"][2 * c + hc], saug_s[hc][:],
                                                   waits=[ev_s1]))

                ev_qk_evac = _p_state[c]

                ev_y_tile = {1: [], 2: [], 3: [], 4: []}
                ev_xt_tl = {}
                for tl in (1, 2):
                    tabs = 4 * c + tl
                    ev_xt_tl[tl] = K.dma("sync", xtm_s[tl - 1][:],
                                         xTM_d[128 * (tabs - 1):128 * tabs, :],
                                         waits=takefree(f"xtm{tl - 1}"),
                                         sem=f"xt{tl - 1}")

                def emit_rearr(g, buf=buf, ev_qk_evac=ev_qk_evac):
                    gpar = g % 2
                    r0 = 64 * (g % 2)
                    fbq = g // 2
                    fbk = 2 + g // 2
                    wqh = takefree(f"qkh{gpar}")
                    wkh = takefree(f"kkh{gpar}")
                    ev_kh = None
                    for mm_ in range(4):
                        K.dma("sync", qkh_s[gpar][:, 640 * mm_:640 * (mm_ + 1)],
                              qkfm_s[buf][r0 + 16 * mm_:r0 + 16 * (mm_ + 1),
                                          640 * fbq:640 * (fbq + 1)],
                              waits=(ev_qk_evac + wqh + wkh) if mm_ == 0 else (),
                              sem=f"qh{gpar}")
                        ev_kh = K.dma("sync", kkh_s[gpar][:, 640 * mm_:640 * (mm_ + 1)],
                                      qkfm_s[buf][r0 + 16 * mm_:r0 + 16 * (mm_ + 1),
                                                  640 * fbk:640 * (fbk + 1)],
                                      sem=f"qh{gpar}")
                    use(f"qkfm{buf}", ev_kh)
                    return [ev_kh]

                ev_qkh_g = {0: emit_rearr(0)}
                for g in range(4):
                    gpar = g % 2
                    ev_qkh = ev_qkh_g[g]
                    wphi = takefree(f"phiqq{gpar}")
                    ev_prod_all = []
                    for hg in range(4):
                        qrow = qkh_s[gpar][:, 640 * hg + 128: 640 * hg + 640]
                        wq0 = ev_qkh + wphi + [ev_ones]
                        evA0 = K.mm(2, lambda e, f, l, qrow=qrow: e.matmul(
                            pb[2][:, 0:512], repm_s[0:16, 0:128], qrow,
                            start=f, stop=l), first=True, last=True, waits=wq0)
                        evA1 = K.mm(3, lambda e, f, l, qrow=qrow: e.matmul(
                            pb[3][:, 0:512], repm_s[0:16, 128:256], qrow,
                            start=f, stop=l), first=True, last=True)
                        evB = K.mm(4, lambda e, f, l, qrow=qrow: e.matmul(
                            pb[4][:, 0:512], repm_s[0:16, 256:384], qrow,
                            start=f, stop=l), first=True, last=True)
                        evqb = K.act("scalar", lambda e: e.copy(
                            qb_s[:], pb[4][:, 0:512]), waits=[evB] + takefree("qb"))
                        use("pb4", evqb)
                        ev0 = K.act("vector", lambda e, hg=hg, gpar=gpar: e.tensor_mul(
                            phiqq_s[gpar][:, 512 * (2 * hg): 512 * (2 * hg + 1)],
                            pb[2][:, 0:512], qb_s[:]), waits=[evA0, evqb])
                        ev1 = K.act("vector", lambda e, hg=hg, gpar=gpar: e.tensor_mul(
                            phiqq_s[gpar][:, 512 * (2 * hg + 1): 512 * (2 * hg + 2)],
                            pb[3][:, 0:512], qb_s[:]), waits=[evA1, evqb])
                        use("pb2", ev0)
                        use("pb3", ev1)
                        use("qb", ev0, ev1)
                        ev_prod_all += [ev0, ev1]
                    if g + 1 < 4:
                        ev_qkh_g[g + 1] = emit_rearr(g + 1)
                    if "phiqq" in taps:
                        use(f"phiqq{gpar}", K.tap_dma("tapq", taps["phiqq"][c, g],
                                                      phiqq_s[gpar][:],
                                                      waits=ev_prod_all))
                    SCB = {1: (5, 6), 2: (2, 3), 3: (5, 6), 4: (2, 3)}
                    YB = {1: (0, 1), 2: (4, 7), 3: (0, 1), 4: (4, 7)}
                    st_ae = {}

                    def emit_scores(tl, g=g, gpar=gpar):
                        b5, b6 = SCB[tl]
                        op5 = op6 = False
                        for hg in range(4):
                            kcur = kkh_s[gpar][:, 640 * hg + 128 * tl: 640 * hg + 128 * (tl + 1)]
                            kprv = kkh_s[gpar][:, 640 * hg + 128 * (tl - 1): 640 * hg + 128 * tl]
                            qcur = qkh_s[gpar][:, 640 * hg + 128 * tl: 640 * hg + 128 * (tl + 1)]
                            K.mm(b5, lambda e, f, l, kcur=kcur, qcur=qcur, hg=hg, b5=b5:
                                 e.matmul(pb[b5][:, 128 * hg:128 * (hg + 1)], kcur, qcur,
                                          start=f, stop=l), first=not op5, last=(hg == 3),
                                 waits=ev_qkh if not op5 else ())
                            op5 = True
                            K.mm(b6, lambda e, f, l, kprv=kprv, qcur=qcur, hg=hg, b6=b6:
                                 e.matmul(pb[b6][:, 128 * hg:128 * (hg + 1)], kprv, qcur,
                                          start=f, stop=l), first=not op6, last=(hg == 3))
                            op6 = True
                        return K.now("sT")

                    def emit_ae(tl, ev_sc, g=g):
                        tabs = 4 * c + tl
                        b5, b6 = SCB[tl]
                        has_aprev = ((tl - 1) % 2 == 1)
                        apar = (4 * g + tl) % 2
                        waet = takefree(f"aet{apar}") + takefree(f"ehalo{apar}") + \
                            takefree(f"aprev{apar}")
                        evA = K.act("scalar", lambda e, apar=apar, b5=b5: e.activation(
                            aet_s[apar][:, 0:512], pb[b5][:, 0:512], AF.Square,
                            bias=1.0, scale=1.0), waits=[ev_sc] + waet)
                        evE = K.act("scalar", lambda e, apar=apar, b5=b5: e.activation(
                            aet_s[apar][:, 512:1024], pb[b5][:, 0:512], AF.Exp),
                            waits=[ev_sc])
                        evEh = K.act("scalar", lambda e, apar=apar, b6=b6: e.activation(
                            ehalo_s[apar][:], pb[b6][:, 0:512], AF.Exp),
                            waits=[ev_sc])
                        use(f"pb{b5}", evA, evE)
                        use(f"pb{b6}", evEh)
                        evAp2 = None
                        if has_aprev:
                            evAp = K.act("scalar", lambda e, apar=apar, b6=b6: e.activation(
                                aprev_s[apar][:], pb[b6][:, 0:512], AF.Square,
                                bias=1.0, scale=1.0), waits=[ev_sc])
                            use(f"pb{b6}", evAp)
                            evAp2 = K.act("vector", lambda e, apar=apar: e.tensor_scalar(
                                aprev_s[apar][:], aprev_s[apar][:],
                                0.5, 0.5, ALU.mult, ALU.add), waits=[evAp])
                        mA = mcA_s[:].unsqueeze(1).broadcast_to([128, 4, 128])
                        mC = mbC_s[:].unsqueeze(1).broadcast_to([128, 4, 128])
                        mh_t = mbH0_s if tabs == 1 else mbH_s
                        mH = mh_t[:].unsqueeze(1).broadcast_to([128, 4, 128])
                        evA2 = K.act("vector", lambda e, apar=apar: e.tensor_scalar(
                            aet_s[apar][:, 0:512], aet_s[apar][:, 0:512],
                            0.5, 0.5, ALU.mult, ALU.add), waits=[evA])
                        evA3 = K.act("vector", lambda e, apar=apar, mA=mA: e.tensor_mul(
                            aet_s[apar][:, 0:512].rearrange("p (g q) -> p g q", g=4),
                            aet_s[apar][:, 0:512].rearrange("p (g q) -> p g q", g=4),
                            mA), waits=[evA2])
                        evE2 = K.act("vector", lambda e, apar=apar, mC=mC: e.tensor_mul(
                            aet_s[apar][:, 512:1024].rearrange("p (g q) -> p g q", g=4),
                            aet_s[apar][:, 512:1024].rearrange("p (g q) -> p g q", g=4),
                            mC), waits=[evE])
                        evEh2 = K.act("vector", lambda e, apar=apar, mH=mH: e.tensor_mul(
                            ehalo_s[apar][:].rearrange("p (g q) -> p g q", g=4),
                            ehalo_s[apar][:].rearrange("p (g q) -> p g q", g=4),
                            mH), waits=[evEh])
                        return (apar, evA3, evE2, evEh2, evAp2, has_aprev)

                    def emit_v(tl, st, g=g, gpar=gpar):
                        tabs = 4 * c + tl
                        slot = tabs % 5
                        pslot = (tabs - 1) % 5
                        hcpar = (tl - 1) // 2
                        apar, evA3, evE2, evEh2, evAp2, has_aprev = st
                        b0, b1 = YB[tl]
                        op0 = op1 = False
                        for hg in range(4):
                            h = 4 * g + hg
                            qcur = qkh_s[gpar][:, 640 * hg + 128 * tl: 640 * hg + 128 * (tl + 1)]
                            K.mm(b0, lambda e, f, l, apar=apar, hg=hg, slot=slot, h=h, b0=b0:
                                 e.matmul(pb[b0][:, 65 * hg:65 * hg + 65],
                                          aet_s[apar][:, 128 * hg:128 * (hg + 1)],
                                          vring_s[slot][:, 65 * h:65 * h + 65],
                                          start=f, stop=l),
                                 first=not op0, last=False,
                                 waits=([ev_saug[hcpar], evA3, ev_vload.get(tabs)])
                                 if not op0 else ())
                            op0 = True
                            if has_aprev:
                                K.mm(b0, lambda e, f, l, apar=apar, hg=hg, pslot=pslot, h=h, b0=b0:
                                     e.matmul(pb[b0][:, 65 * hg:65 * hg + 65],
                                              aprev_s[apar][:, 128 * hg:128 * (hg + 1)],
                                              vring_s[pslot][:, 65 * h:65 * h + 65],
                                              start=f, stop=l),
                                     first=False, last=False,
                                     waits=[evAp2, ev_vload.get(tabs - 1)] if hg == 0 else ())
                            K.mm(b0, lambda e, f, l, h=h, hg=hg, hcpar=hcpar, b0=b0: e.matmul(
                                pb[b0][:, 65 * hg:65 * hg + 65], ones_row[:],
                                zrow_s[hcpar][0:1, 65 * h: 65 * h + 65],
                                start=f, stop=l), first=False, last=False,
                                waits=[ev_zrow[hcpar]] if hg == 0 else ())
                            K.mm(b0, lambda e, f, l, h=h, hg=hg, qcur=qcur, hcpar=hcpar, b0=b0:
                                 e.matmul(
                                     pb[b0][:, 65 * hg:65 * hg + 65], qcur,
                                     saug_s[hcpar][0:16, 2 * HB + 65 * h: 2 * HB + 65 * h + 65],
                                     start=f, stop=l), first=False, last=False)
                            K.mm(b0, lambda e, f, l, h=h, hg=hg, gpar=gpar, tl=tl, hcpar=hcpar, b0=b0:
                                 e.matmul(
                                     pb[b0][:, 65 * hg:65 * hg + 65],
                                     phiqq_s[gpar][:, 512 * (2 * hg) + 128 * (tl - 1):
                                                   512 * (2 * hg) + 128 * tl],
                                     saug_s[hcpar][:, 0 * HB + 65 * h: 0 * HB + 65 * h + 65],
                                     start=f, stop=l), first=False, last=False,
                                 waits=ev_prod_all if hg == 0 else ())
                            K.mm(b0, lambda e, f, l, h=h, hg=hg, gpar=gpar, tl=tl, hcpar=hcpar, b0=b0:
                                 e.matmul(
                                     pb[b0][:, 65 * hg:65 * hg + 65],
                                     phiqq_s[gpar][:, 512 * (2 * hg + 1) + 128 * (tl - 1):
                                                   512 * (2 * hg + 1) + 128 * tl],
                                     saug_s[hcpar][:, 1 * HB + 65 * h: 1 * HB + 65 * h + 65],
                                     start=f, stop=l), first=False, last=(hg == 3))
                            K.mm(b1, lambda e, f, l, apar=apar, hg=hg, slot=slot, h=h, b1=b1:
                                 e.matmul(pb[b1][:, 65 * hg:65 * hg + 65],
                                          aet_s[apar][:, 512 + 128 * hg:512 + 128 * (hg + 1)],
                                          vring_s[slot][:, 65 * h:65 * h + 65],
                                          start=f, stop=l),
                                 first=not op1, last=False,
                                 waits=[evE2] if not op1 else ())
                            op1 = True
                            K.mm(b1, lambda e, f, l, apar=apar, hg=hg, pslot=pslot, h=h, b1=b1:
                                 e.matmul(pb[b1][:, 65 * hg:65 * hg + 65],
                                          ehalo_s[apar][:, 128 * hg:128 * (hg + 1)],
                                          vring_s[pslot][:, 65 * h:65 * h + 65],
                                          start=f, stop=l),
                                 first=False, last=(hg == 3),
                                 waits=[evEh2, ev_vload.get(tabs - 1)] if hg == 0 else ())
                        ev_vi = K.now("sT")
                        use(f"aet{apar}", ev_vi)
                        use(f"ehalo{apar}", ev_vi)
                        use(f"aprev{apar}", ev_vi)
                        use(f"vring{pslot}", ev_vi)
                        return ev_vi

                    def emit_evac(tl, ev_vi, g=g):
                        b0, b1 = YB[tl]
                        wd = takefree(f"dens{tl % 2}")
                        dlin = bass.AP(pb[b0], 64, [[512, 128], [65, 4]])
                        dloc = bass.AP(pb[b1], 64, [[512, 128], [65, 4]])
                        dc = 8 * (tl % 2)
                        ed1 = K.act("vector", lambda e, dlin=dlin, dc=dc: e.tensor_scalar(
                            dens_s[:, dc:dc + 4], dlin, 1e-8, None, ALU.add),
                            waits=[ev_vi] + wd)
                        ed2 = K.act("vector", lambda e, dloc=dloc, dc=dc: e.tensor_copy(
                            dens_s[:, 16 + dc:16 + dc + 4], dloc), waits=[ev_vi])
                        er1 = K.act("vector", lambda e, dc=dc: e.reciprocal(
                            rden_s[:, dc:dc + 4], dens_s[:, dc:dc + 4]), waits=[ed1])
                        er2 = K.act("vector", lambda e, dc=dc: e.reciprocal(
                            rden_s[:, 16 + dc:16 + dc + 4], dens_s[:, 16 + dc:16 + dc + 4]),
                            waits=[ed2])
                        use(f"dens{tl % 2}", er1, er2)
                        use(f"pb{b0}", ed1)
                        use(f"pb{b1}", ed2)
                        wyb = takefree(f"y{tl - 1}") if g == 0 else []
                        for hg in range(4):
                            h = 4 * g + hg
                            e_sc = K.act("scalar", lambda e, h=h, hg=hg, tl=tl, b0=b0, dc=dc:
                                         e.activation(
                                             y_s[tl - 1][:, 64 * h:64 * h + 64],
                                             pb[b0][:, 65 * hg:65 * hg + 64], AF.Identity,
                                             scale=rden_s[:, dc + hg:dc + hg + 1]),
                                         waits=[er1] + wyb)
                            e_lo = K.act("vector", lambda e, h=h, hg=hg, tl=tl, b1=b1, dc=dc:
                                         e.scalar_tensor_tensor(
                                             y_s[tl - 1][:, 64 * h:64 * h + 64],
                                             pb[b1][:, 65 * hg:65 * hg + 64],
                                             rden_s[:, 16 + dc + hg:16 + dc + hg + 1],
                                             y_s[tl - 1][:, 64 * h:64 * h + 64],
                                             ALU.mult, ALU.add), waits=[er2, e_sc])
                            use(f"pb{b0}", e_sc)
                            use(f"pb{b1}", e_lo)
                            ev_y_tile[tl].append(e_lo)

                    ev_sc_t = {1: emit_scores(1)}
                    ev_sc_t[2] = emit_scores(2)
                    st_ae[1] = emit_ae(1, ev_sc_t[1])
                    ev_v1 = emit_v(1, st_ae[1])
                    ev_sc_t[3] = emit_scores(3)
                    st_ae[2] = emit_ae(2, ev_sc_t[2])
                    ev_v2 = emit_v(2, st_ae[2])
                    emit_evac(1, ev_v1)
                    ev_sc_t[4] = emit_scores(4)
                    st_ae[3] = emit_ae(3, ev_sc_t[3])
                    ev_v3 = emit_v(3, st_ae[3])
                    emit_evac(2, ev_v2)
                    st_ae[4] = emit_ae(4, ev_sc_t[4])
                    ev_v4 = emit_v(4, st_ae[4])
                    emit_evac(3, ev_v3)
                    emit_evac(4, ev_v4)
                    use(f"qkh{gpar}", K.now("sT"))
                    use(f"kkh{gpar}", K.now("sT"))
                    use(f"phiqq{gpar}", K.now("sT"))

                ev_xn_tl = {}
                for tl in range(1, 5):
                    tabs = 4 * c + tl
                    tpar = tl % 2
                    if tl in ev_xt_tl:
                        ev_xt = ev_xt_tl[tl]
                    else:
                        ev_xt = K.dma("sync", xtm_s[(tl - 1) % 2][:],
                                      xTM_d[128 * (tabs - 1):128 * tabs, :],
                                      waits=takefree(f"xtm{(tl - 1) % 2}"),
                                      sem=f"xt{(tl - 1) % 2}")
                    ev_ytap = None
                    if "y" in taps:
                        ev_ytap = K.tap_dma("tapy", taps["y"][128 * (tabs - 1):128 * tabs, :],
                                            y_s[tl - 1][:], waits=ev_y_tile[tl])
                    ev_r1 = K.act("vector", lambda e, tl=tl: e.tensor_add(
                        y_s[tl - 1][:], y_s[tl - 1][:], xtm_s[(tl - 1) % 2][:]),
                        waits=ev_y_tile[tl] + [ev_xt, ev_ytap])
                    use(f"xtm{(tl - 1) % 2}", ev_r1)
                    ev_r1d = K.dma("gpsimd", r1_dram[128 * (tabs - 1):128 * tabs, :],
                                   y_s[tl - 1][:], waits=[ev_r1], sem=f"ylg{tl - 1}")
                    if "r1" in taps:
                        use(f"y{tl - 1}",
                            K.tap_dma("tapy", taps["r1"][128 * (tabs - 1):128 * tabs, :],
                                      y_s[tl - 1][:], waits=[ev_r1]))
                    ev_sum = K.act("vector", lambda e, tl=tl: e.reduce_sum(
                        vstat_s[tl % 2][:, 0:1], y_s[tl - 1][:], AX.X),
                        waits=[ev_r1] + takefree(f"vstat{tl % 2}"))
                    wxn = takefree(f"xnt{tl - 1}")
                    ev_ssq = K.act("scalar", lambda e, tl=tl: e.activation(
                        xnt_s[tl - 1][:], y_s[tl - 1][:], AF.Square,
                        accum_out=vstat_s[tl % 2][:, 1:2]), waits=[ev_r1] + wxn)
                    e_scale, e_bias = ln_math(tl % 2, ev_sum, ev_ssq, with_rms=True)
                    ev_xn = K.act("scalar", lambda e, tl=tl: e.activation(
                        xnt_s[tl - 1][:], y_s[tl - 1][:], AF.Identity,
                        scale=vstat_s[tl % 2][:, 13:14], bias=vstat_s[tl % 2][:, 14:15]),
                        waits=[e_scale, e_bias, ev_ssq])
                    ev_xn_tl[tl] = ev_xn
                    use(f"y{tl - 1}", ev_xn, ev_r1d, ev_sum)
                    use(f"vstat{tl % 2}", ev_xn)
                    if "xn" in taps:
                        use(f"xnt{tl - 1}",
                            K.tap_dma("tapx", taps["xn"][128 * (tabs - 1):128 * tabs, :],
                                      xnt_s[tl - 1][:], waits=[ev_xn]))

                _c_state[c] = dict(buf=buf, ev_xnf_parts=None,
                                   ld_tiles=ld_tiles, ev_xn_tl=ev_xn_tl)

            def emit_trans(c):
                ev_xn_tl = _c_state[c]['ev_xn_tl']
                ev_xnf_parts = []
                wtr = takefree(f"xnf{c % 2}")
                for tl in range(1, 5):
                    pbt = pb[2][:, 0:512].bitcast(BF16)
                    for k in range(8):
                        K.trans(2, lambda e, tl=tl, k=k, pbt=pbt: e.transpose(
                            pbt[:, 128 * k:128 * (k + 1)],
                            xnt_s[tl - 1][:, 128 * k:128 * (k + 1)], idenb_s[:]),
                            waits=([ev_xn_tl[tl]] + wtr) if k == 0 else ())
                    ev_tr = K.now("sT")
                    o2 = xnf_s[c % 2][:].rearrange("p (k t) -> p k t", k=8)[:, :, 128 * (tl - 1):128 * tl]
                    i2 = pbt.rearrange("p (k t) -> p k t", k=8)
                    ev1 = K.act("vector", lambda e, o2=o2, i2=i2: e.tensor_copy(o2, i2),
                                waits=[ev_tr])
                    use("pb2", ev1)
                    ev_xnf_parts += [ev1]
                    use(f"xnt{tl - 1}", ev1)
                if "xnf" in taps:
                    use(f"xnf{c % 2}", K.tap_dma("tapx", taps["xnf"][c], xnf_s[c % 2][:],
                                         waits=ev_xnf_parts))
                _c_state[c]['ev_xnf_parts'] = ev_xnf_parts

            def ffn_stage(c):
                st = _c_state[c]
                buf = st['buf']
                ev_xn_tl = st['ev_xn_tl']
                ld_tiles = st['ld_tiles']
                ev_xnf_parts = st['ev_xnf_parts']
                ev_gelu_all = []
                for f in range(32):
                    wb = wctr[0] % 4
                    wctr[0] += 1
                    ev_wl = K.dma("sync", wstr_s[wb][:],
                                  wtu_dram[:, 1024 * f:1024 * (f + 1)],
                                  waits=takefree(f"wstr{wb}") +
                                  ([ev_wtu] if c == 0 and f < 4 else []), sem=f"ws{wb}")
                    bank = 4 + f % 2
                    for k in range(8):
                        K.mm(bank, lambda e, ff, ll, k=k, wb=wb, bank=bank: e.matmul(
                            pb[bank][:, 0:512],
                            wstr_s[wb][:, 128 * k:128 * (k + 1)],
                            xnf_s[c % 2][:, 512 * k:512 * (k + 1)], start=ff, stop=ll),
                            first=(k == 0), last=(k == 7),
                            waits=(ev_xnf_parts + [ev_wl]) if k == 0 else ())
                    ev_up = K.now("sT")
                    use(f"wstr{wb}", ev_up)
                    ev_g = K.act("scalar", lambda e, f=f, bank=bank: e.activation(
                        hfm_s[:, 512 * f:512 * (f + 1)], pb[bank][:, 0:512], AF.Gelu),
                        waits=[ev_up] + (takefree("hfm") if f == 0 else []))
                    use(f"pb{bank}", ev_g)
                    ev_gelu_all.append(ev_g)
                ev_gelu = ev_gelu_all[-1]
                if "hf" in taps:
                    use("hfm", K.tap_dma("tapx", taps["hf"][c], hfm_s[:],
                                         waits=[ev_gelu]))
                ev_rmsh = {}
                for tl in range(1, 5):
                    tpar = tl % 2
                    for f in range(32):
                        hsl = hfm_s[:, 512 * f + 128 * (tl - 1): 512 * f + 128 * tl]
                        K.mm(6, lambda e, ff, ll, hsl=hsl: e.matmul(
                            pb[6][:, 0:128], hsl, hsl, start=ff, stop=ll),
                            first=(f == 0), last=(f == 31),
                            waits=[ev_gelu] if f == 0 else ())
                    ev_gr = K.now("sT")
                    wvs = takefree(f"vstat{tpar}")
                    ev_dm = K.act("vector", lambda e: e.tensor_mul(
                        gsc_s[:], pb[6][:, 0:128], idmask_s[:]),
                        waits=[ev_gr, ev_css] + takefree("gsc"))
                    use("pb6", ev_dm)
                    ev_dg = K.act("vector", lambda e, tpar=tpar: e.reduce_sum(
                        vstat_s[tpar][:, 1:2], gsc_s[:], AX.X),
                        waits=[ev_dm] + wvs)
                    use("gsc", ev_dg)
                    ev_s = K.act("scalar", lambda e, tpar=tpar: e.activation(
                        vstat_s[tpar][:, 2:3], vstat_s[tpar][:, 1:2], AF.Sqrt),
                        waits=[ev_dg])
                    ev_m = K.act("vector", lambda e, tpar=tpar: e.tensor_scalar(
                        vstat_s[tpar][:, 3:4], vstat_s[tpar][:, 2:3], 1.0 / 64.0, 1e-8,
                        ALU.mult, ALU.add), waits=[ev_s])
                    ev_rmsh[tl] = K.act("vector", lambda e, tpar=tpar, tl=tl: e.reciprocal(
                        rmsi_s[:, tl - 1:tl], vstat_s[tpar][:, 3:4]),
                        waits=[ev_m] + takefree(f"rmsi{tl - 1}"))
                    use(f"vstat{tpar}", ev_rmsh[tl])
                ln2_items = []
                for p2 in range(2):
                    tls = [2 * p2 + 1, 2 * p2 + 2]
                    banks = {tls[0]: (0, 1), tls[1]: (2, 3)}
                    for f in range(32):
                        wb = wctr[0] % 4
                        wctr[0] += 1
                        ev_wl = K.dma("sync", wstr_s[wb][:],
                                      wtd_dram[:, 1024 * f:1024 * (f + 1)],
                                      waits=takefree(f"wstr{wb}") +
                                      ([ev_wtd] if c == 0 and p2 == 0 and f < 4 else []),
                                      sem=f"ws{wb}")
                        for tl in tls:
                            b0, b1 = banks[tl]
                            hsl = hfm_s[:, 512 * f + 128 * (tl - 1): 512 * f + 128 * tl]
                            K.mm(b0, lambda e, ff, ll, hsl=hsl, wb=wb, b0=b0: e.matmul(
                                pb[b0][:, 0:512], hsl, wstr_s[wb][:, 0:512],
                                start=ff, stop=ll), first=(f == 0), last=(f == 31),
                                waits=([ev_wl, ev_gelu]) if f == 0 else [ev_wl])
                            K.mm(b1, lambda e, ff, ll, hsl=hsl, wb=wb, b1=b1: e.matmul(
                                pb[b1][:, 0:512], hsl, wstr_s[wb][:, 512:1024],
                                start=ff, stop=ll), first=(f == 0), last=(f == 31))
                        use(f"wstr{wb}", K.now("sT"))
                    ev_dn_all = K.now("sT")
                    er2_t = {}
                    for tl in tls:
                        tabs = 4 * c + tl
                        b0, b1 = banks[tl]
                        r2b = y_s[tl - 1]
                        ev_r1l = K.dma("sync", r2b[:],
                                       r1_dram[128 * (tabs - 1):128 * tabs, :],
                                       waits=takefree(f"y{tl - 1}"), sem=f"yl{tl - 1}")
                        er2a = K.act("vector", lambda e, b0=b0, tl=tl, r2b=r2b:
                                     e.scalar_tensor_tensor(
                                         r2b[:, 0:512], pb[b0][:, 0:512],
                                         rmsi_s[:, tl - 1:tl], r2b[:, 0:512],
                                         ALU.mult, ALU.add),
                                     waits=[ev_dn_all, ev_r1l, ev_rmsh[tl]])
                        er2b = K.act("vector", lambda e, b1=b1, tl=tl, r2b=r2b:
                                     e.scalar_tensor_tensor(
                                         r2b[:, 512:1024], pb[b1][:, 0:512],
                                         rmsi_s[:, tl - 1:tl], r2b[:, 512:1024],
                                         ALU.mult, ALU.add), waits=[ev_dn_all, ev_r1l])
                        use(f"rmsi{tl - 1}", er2a, er2b)
                        use(f"pb{b0}", er2a)
                        use(f"pb{b1}", er2b)
                        er2_t[tl] = (er2a, er2b)

                    def emit_ln2(tl, c=c, er2_t=er2_t):
                        tabs = 4 * c + tl
                        tpar = tl % 2
                        r2b = y_s[tl - 1]
                        er2a, er2b = er2_t[tl]
                        wvs = takefree(f"vstat{tpar}")
                        ev_sum2 = K.act("vector", lambda e, tpar=tpar, r2b=r2b: e.reduce_sum(
                            vstat_s[tpar][:, 0:1], r2b[:], AX.X),
                            waits=[er2a, er2b] + wvs)
                        wout = takefree(f"xtm{(tl - 1) % 2}")
                        ev_ssq2 = K.act("scalar", lambda e, tpar=tpar, tl=tl, r2b=r2b: e.activation(
                            xtm_s[(tl - 1) % 2][:], r2b[:], AF.Square,
                            accum_out=vstat_s[tpar][:, 1:2]),
                            waits=[er2a, er2b] + wout)
                        e_sc2, e_bi2 = ln_math(tpar, ev_sum2, ev_ssq2, with_rms=False)
                        ev_o = K.act("scalar", lambda e, tpar=tpar, tl=tl, r2b=r2b: e.activation(
                            xtm_s[(tl - 1) % 2][:], r2b[:], AF.Identity,
                            scale=vstat_s[tpar][:, 13:14], bias=vstat_s[tpar][:, 14:15]),
                            waits=[e_sc2, e_bi2, ev_ssq2])
                        use(f"y{tl - 1}", ev_o)
                        use(f"vstat{tpar}", ev_o)
                        ev_od = K.dma("sync", out_d[128 * (tabs - 1):128 * tabs, :],
                                      xtm_s[(tl - 1) % 2][:], waits=[ev_o],
                                      sem=f"xt{(tl - 1) % 2}")
                        use(f"xtm{(tl - 1) % 2}", ev_od)
                    for tl in tls:
                        ln2_items.append(lambda tl=tl, f=emit_ln2: f(tl))
                    if p2 == 1:
                        for it in ln2_items:
                            it()
                use("hfm", ev_dn_all)
                use(f"xnf{c % 2}", K.now("sT"))
                use(f"xfm{buf}", K.now("sT"))
                for hc_ in range(2):
                    use(f"saug{hc_}", K.now("sT"))
                    use(f"zrow{hc_}", K.now("sT"))
                use(f"qkfm{buf}", K.now("sT"))

            for c in range(NCH):
                proj_stage(c)
                attn_stage(c)
                if c >= 1:
                    ffn_stage(c - 1)
                emit_trans(c)
            ffn_stage(NCH - 1)
            finalize()

        block = ctx.enter_context(nc.Block())
        for eng_name in ENGS:
            def body(eng, _ops=K.ops[eng_name]):
                for fn in _ops:
                    fn(eng)
            getattr(block, eng_name)(body)

    return nc, taps


TCORE = 4096


def _wka(Wk):
    wka = np.zeros((1024, 272), np.float32)
    wt = Wk.T
    for h in range(16):
        wka[:, 17 * h:17 * h + 16] = wt[:, 16 * h:16 * (h + 1)]
    return wka


def _bkra(bk):
    b = np.zeros((1, 272), np.float32)
    for h in range(16):
        b[0, 17 * h:17 * h + 16] = bk[16 * h:16 * (h + 1)]
        b[0, 17 * h + 16] = 1.0
    return b


def _host_inputs(x_main, x_halo, Wq, bq, Wk, bk, Wv, bv, w_up, w_down, pos, full):
    x_main = np.ascontiguousarray(x_main, dtype=np.float32)
    if x_halo is None:
        x_halo = np.zeros((64, D), np.float32)
    xT_cols = np.concatenate([np.zeros((64, D), np.float32),
                              np.asarray(x_halo, np.float32), x_main], axis=0)
    ins = {
        "xT": np.ascontiguousarray(xT_cols.T),
        "wqk": np.ascontiguousarray(np.concatenate([Wq.T, Wk.T], axis=1)),
        "bkra": _bkra(bk),
        "wka": _wka(Wk),
    }
    wva = np.zeros((D, HB), np.float32)
    bvar = np.zeros((1, HB), np.float32)
    for h in range(H):
        wva[:, 65 * h:65 * h + 64] = Wv.T[:, 64 * h:64 * h + 64]
        bvar[0, 65 * h:65 * h + 64] = bv[64 * h:64 * h + 64]
        bvar[0, 65 * h + 64] = 1.0
    ins["wva"] = wva
    ins["bvar"] = bvar
    if not full:
        return ins
    ins["xTM"] = x_main
    ins["bqk"] = np.concatenate([bq, bk]).reshape(512, 1).astype(np.float32)
    ins["wup_sw"] = np.ascontiguousarray(
        w_up.T.reshape(8, 128, 32, 128).transpose(1, 2, 0, 3).reshape(128, 8 * DFF))
    ins["wdn_sw"] = np.ascontiguousarray(
        w_down.T.reshape(32, 128, 1024).transpose(1, 0, 2).reshape(128, 32 * 1024))
    kk, qq = np.meshgrid(np.arange(128), np.arange(128), indexing="ij")
    ins["mcA"] = (kk <= qq).astype(np.float32)
    ins["mbC"] = ((kk <= qq) & (kk >= qq - 63)).astype(np.float32)
    mbHx = (kk >= qq + 65).astype(np.float32)
    ins["mbH"] = mbHx
    ins["mbH0"] = mbHx if pos > 0 else np.zeros((128, 128), np.float32)
    msk = np.zeros((128, 4), np.float32)
    msk[:, :pos] = 1.0
    ins["mskv"] = msk
    ins["iden"] = np.eye(128, dtype=np.float32)
    rep = np.zeros((16, 3 * 128), np.float32)
    for p in range(128):
        if p // 16 < 8:
            rep[p // 16, p] = 1.0
        rep[8 + p // 16, 128 + p] = 1.0
        rep[p % 16, 256 + p] = 1.0
    ins["repm"] = rep
    return ins


def kernel(L0_attn_x, L0_attn_Wq, L0_attn_bq, L0_attn_Wk, L0_attn_bk,
           L0_attn_Wv, L0_attn_bv, L0_ffn_w_up, L0_ffn_w_down):
    x = np.ascontiguousarray(np.asarray(L0_attn_x, dtype=np.float32))
    Wq = np.asarray(L0_attn_Wq, np.float32)
    bq = np.asarray(L0_attn_bq, np.float32)
    Wk = np.asarray(L0_attn_Wk, np.float32)
    bk = np.asarray(L0_attn_bk, np.float32)
    Wv = np.asarray(L0_attn_Wv, np.float32)
    bv = np.asarray(L0_attn_bv, np.float32)
    w_up = np.asarray(L0_ffn_w_up, np.float32)
    w_down = np.asarray(L0_ffn_w_down, np.float32)
    B, N, _ = x.shape
    assert (B, N) == (2, 16384)

    ins_full = []
    for c in range(8):
        b, p = c // 4, c % 4
        xm = x[b, TCORE * p:TCORE * (p + 1)]
        xh = None if p == 0 else x[b, TCORE * p - 64:TCORE * p]
        ins_full.append(_host_inputs(xm, xh, Wq, bq, Wk, bk, Wv, bv,
                                     w_up, w_down, p, True))

    nc2, _ = build_program(T=TCORE, mode="full", fused_cc=True)
    r2 = run_bass_kernel_spmd(nc2, ins_full, core_ids=list(range(8)))

    out = np.empty((2, 16384, 1024), np.float32)
    for c in range(8):
        b, p = c // 4, c % 4
        out[b, TCORE * p:TCORE * (p + 1)] = np.asarray(r2.results[c]["out"])
    return out
